# version 16
# baseline (speedup 1.0000x reference)
import contextlib
import numpy as np

import concourse.bass as bass
import concourse.mybir as mybir
from concourse.bass_utils import run_bass_kernel_spmd

F32 = mybir.dt.float32
BF16 = mybir.dt.bfloat16
AF = mybir.ActivationFunctionType
ALU = mybir.AluOpType
AX = mybir.AxisListType

S, D, FF, NH, HD = 2048, 1024, 4096, 16, 64
ND, NF, NT = D // 128, FF // 128, S // 128
G, GS = 4, 512
NTG = GS // 128
INV_D = 1.0 / D
INV_FF = 1.0 / FF
EPS_OUT = 1e-5
EPS_IN = 1e-12
_S32 = np.float64(np.float32(np.sqrt(np.float32(2.0))))
_S64 = np.sqrt(np.float64(2.0))
KAPPA = float(np.float32((_S64 - _S32) / _S32))
KAPPA2 = float(np.float32((_S64 - _S32) / _S64))

DEBUG_TAP = None
BUILD_PHASES = 99


RR = {"ld": 4, "gq": 2}


class Bld:
    def __init__(self, nc, block):
        self.nc, self.block = nc, block
        self.sem, self.cnt = {}, {}
        self.snap = {}
        self.rri = {}

    def add_sem(self, name, handle):
        self.sem[name] = handle
        self.cnt[name] = 0

    def add_rr(self, base, handles):
        for j, h in enumerate(handles):
            self.add_sem(f"{base}{j}", h)
        self.cnt[base] = 0
        self.snap[base] = {0: tuple(0 for _ in handles)}
        self.rri[base] = 0

    def dma(self, eng, base, out, in_, **kw):
        j = self.rri[base] % RR[base]
        self.rri[base] += 1
        s = f"{base}{j}"
        self.wait(eng, s)
        instr = eng.dma_start(out=out, in_=in_, **kw)
        instr.then_inc(self.sem[s], 16)
        self.cnt[s] += 16
        self.cnt[base] += 16
        self.snap[base][self.cnt[base]] = tuple(
            self.cnt[f"{base}{k}"] for k in range(RR[base]))
        return self.cnt[base]

    def inc(self, instr, name, n=1):
        assert name not in self.snap
        instr.then_inc(self.sem[name], n)
        self.cnt[name] += n
        return self.cnt[name]

    def wait(self, eng, name, val=None):
        if val is None:
            val = self.cnt[name]
        assert val <= self.cnt[name], \
            f"wait {name}>={val} but only {self.cnt[name]} emitted"
        if val == 0:
            return
        if name in self.snap:
            for k, c in enumerate(self.snap[name][val]):
                if c > 0:
                    eng.wait_ge(self.sem[f"{name}{k}"], c)
            return
        eng.wait_ge(self.sem[name], val)


def build_program(tap=None, phases=99):
    nc = bass.Bass("TRN2", target_bir_lowering=False)

    x_d = nc.dram_tensor("x", [S, D], F32, kind="ExternalInput").ap()
    wq_d = nc.dram_tensor("wq", [D, D], F32, kind="ExternalInput").ap()
    wk_d = nc.dram_tensor("wk", [D, D], F32, kind="ExternalInput").ap()
    wv_d = nc.dram_tensor("wv", [D, D], F32, kind="ExternalInput").ap()
    wo_d = nc.dram_tensor("wo", [D, D], F32, kind="ExternalInput").ap()
    w1t_d = nc.dram_tensor("w1t", [NF, ND, 128, 128], F32, kind="ExternalInput").ap()
    w2_d = nc.dram_tensor("w2", [FF, D], F32, kind="ExternalInput").ap()
    wfc_d = nc.dram_tensor("wfc", [FF, D], F32, kind="ExternalInput").ap()
    wpj_d = nc.dram_tensor("wpj", [D, FF], F32, kind="ExternalInput").ap()
    u_d = nc.dram_tensor("u", [FF, 16], F32, kind="ExternalInput").ap()
    vw_d = nc.dram_tensor("vw", [16, D], F32, kind="ExternalInput").ap()
    fb_d = nc.dram_tensor("fb", [16, 16], F32, kind="ExternalInput").ap()
    gate_d = nc.dram_tensor("gate", [1, 1], F32, kind="ExternalInput").ap()
    ident_d = nc.dram_tensor("ident_in", [128, 128], F32, kind="ExternalInput").ap()
    bias_d = {}
    for nm, n in [("bq", D), ("bk", D), ("bv", D), ("bo", D), ("b1", FF),
                  ("b2", D), ("g1", D), ("be1", D), ("g2", D), ("be2", D)]:
        bias_d[nm] = nc.dram_tensor(nm, [n], F32, kind="ExternalInput").ap()
    y_d = nc.dram_tensor("y", [S, D], F32, kind="ExternalOutput").ap()
    sfc_d = nc.dram_tensor("spill_fc", [NF, ND, 128, 128], BF16).ap()
    w1p_d = nc.dram_tensor("spill_w1p", [2, NF, ND, 128, 128], BF16).ap()
    w2p_d = nc.dram_tensor("spill_w2p", [2, NF, 128, D], BF16).ap()
    spj_d = nc.dram_tensor("spill_pj", [2, NF, 4, 128, 128], BF16).ap()
    tap_d = None
    if tap is not None:
        tap_d = nc.dram_tensor("tap", [128, ND * S], F32, kind="ExternalOutput").ap()

    with contextlib.ExitStack() as ctx:
        en = ctx.enter_context

        A_t = en(nc.sbuf_tensor("strA", [128, ND * S], F32))
        B16 = en(nc.sbuf_tensor("strB16", [128, 2 * ND * S], BF16))
        ident = en(nc.sbuf_tensor("ident", [128, 128], F32))
        ident_b = en(nc.sbuf_tensor("ident_b", [128, 128], BF16))
        ones_c = en(nc.sbuf_tensor("ones_c", [128, 1], F32))
        ones_r = en(nc.sbuf_tensor("ones_r", [33, 128], F32))
        pcol = en(nc.sbuf_tensor("pcol", [128, 6 * ND + NF], F32))
        PC = {}
        off = 0
        for nm, n in [("bo", ND), ("b2", ND), ("g1", ND), ("be1", ND),
                      ("g2", ND), ("be2", ND), ("b1", NF)]:
            PC[nm] = (off, n)
            off += n
        gate_b = en(nc.sbuf_tensor("gate_b", [128, 1], F32))
        kv_s = en(nc.sbuf_tensor("kv_s", [128, 512], F32))
        kvp16 = en(nc.sbuf_tensor("kvp16", [128, 1024], BF16))
        sc_fc = en(nc.sbuf_tensor("sc_fc", [128, NF], F32))
        scn_fc = en(nc.sbuf_tensor("scn_fc", [128, NF], F32))
        scn_pj = en(nc.sbuf_tensor("scn_pj", [128, ND], F32))
        gs_fc = en(nc.sbuf_tensor("gs_fc", [128, NF], F32))
        sc_pj = en(nc.sbuf_tensor("sc_pj", [128, ND], F32))
        vT_s = en(nc.sbuf_tensor("vT_s", [128, ND * 16], F32))
        fbT = en(nc.sbuf_tensor("fbT", [16, 16], F32))
        rows = en(nc.sbuf_tensor("rows", [64, S], F32))
        epsc = en(nc.sbuf_tensor("epsc", [128, 2], F32))

        ld_sems = [en(nc.semaphore(f"ld{j}")) for j in range(RR["ld"])]
        gq_sems = [en(nc.semaphore(f"gq{j}")) for j in range(RR["gq"])]
        ts_ = en(nc.semaphore("ts"))
        vs = en(nc.semaphore("vs"))
        ss = en(nc.semaphore("ss"))
        block = en(nc.Block())
        b = Bld(nc, block)
        for nm, h in [("t", ts_), ("v", vs), ("s", ss)]:
            b.add_sem(nm, h)
        b.add_rr("ld", ld_sems)
        b.add_rr("gq", gq_sems)

        def A_c(c):
            return A_t[:, c * S:(c + 1) * S]


        kvp_h = kvp16[:, 0:512]
        kvp_l = kvp16[:, 512:1024]

        def B_slot16(i):
            return B16[:, i * 1024:(i + 1) * 1024]

        def Bp_hc(c):
            return B16[:, c * 2 * S: c * 2 * S + S]

        def Bp_lc(c):
            return B16[:, c * 2 * S + S: (c + 1) * 2 * S]

        def pc(nm, c):
            o, n = PC[nm]
            return pcol[:, o + c:o + c + 1]

        def emit_tap(src_ap):
            cv, cs, ct = b.cnt["v"], b.cnt["s"], b.cnt["t"]

            def _(gp):
                b.wait(gp, "v", cv)
                b.wait(gp, "s", cs)
                b.wait(gp, "t", ct)
                b.dma(gp, "gq", out=tap_d[0:src_ap.shape[0], 0:src_ap.shape[-1]],
                                   in_=src_ap)
                b.wait(gp, "gq")
            block.gpsimd(_)

        setup_marks = {}

        def _(sy):
            off = 0
            for nm in ["bo", "b2", "g1", "be1", "g2", "be2", "b1"]:
                nch = PC[nm][1]
                b.dma(sy, "ld", out=A_t[0:nch, off:off + 128],
                                   in_=bias_d[nm].rearrange("(c p) -> c p", p=128))
                off += 128
            b.dma(sy, "ld", out=gate_b[:], in_=gate_d.partition_broadcast(128))
            b.dma(sy, "ld", out=fbT[:], in_=fb_d)
            b.dma(sy, "ld", out=A_t[0:16, 1024:2048], in_=vw_d)
            b.dma(sy, "ld", out=ident[:], in_=ident_d)
        block.sync(_)
        ld_par = b.cnt["ld"]

        def _(v):
            v.memset(ones_c[:], 1.0)
            v.memset(epsc[:, 0:1], EPS_OUT)
            v.memset(epsc[:, 1:2], EPS_IN)
            b.inc(v.memset(ones_r[:], 1.0), "v", 1)
        block.vector(_)

        def _(sc):
            b.wait(sc, "ld", ld_par)
            b.inc(sc.copy(ident_b[:], ident[:]), "s", 1)
        block.scalar(_)

        with nc.psum_tensor("ps_m", [128, 128], F32) as ps_m:
            off = 0
            for nm in ["bo", "b2", "g1", "be1", "g2", "be2", "b1"]:
                nch = PC[nm][1]
                vprev = b.cnt["v"]

                def _(t, nch=nch, off=off, vprev=vprev):
                    b.wait(t, "ld", ld_par)
                    b.wait(t, "v", vprev)
                    b.inc(t.transpose(ps_m[:, 0:nch], A_t[0:nch, off:off + 128],
                                      ident[0:nch, 0:nch]), "t", 1)
                block.tensor(_)
                tv = b.cnt["t"]

                def _(v, nm=nm, nch=nch, tv=tv):
                    b.wait(v, "t", tv)
                    o = PC[nm][0]
                    b.inc(v.tensor_copy(pcol[:, o:o + nch], ps_m[:, 0:nch]), "v", 1)
                    if nm == "b1":
                        b.wait(v, "v")
                        b.inc(v.scalar_tensor_tensor(
                            pcol[:, o:o + nch], ps_m[:, 0:nch], KAPPA,
                            pcol[:, o:o + nch], ALU.mult, ALU.add), "v", 1)
                block.vector(_)
                off += 128
            vprev = b.cnt["v"]

            def _(t, vprev=vprev):
                b.wait(t, "v", vprev)
                b.inc(t.transpose(ps_m[0:16, 0:16], fbT[:], ident[0:16, 0:16]), "t", 1)
            block.tensor(_)
            tv = b.cnt["t"]

            def _(v, tv=tv):
                b.wait(v, "t", tv)
                b.inc(v.tensor_copy(fbT[:], ps_m[0:16, 0:16]), "v", 1)
            block.vector(_)
            for k in range(ND):
                vprev = b.cnt["v"]

                def _(t, k=k, vprev=vprev):
                    b.wait(t, "v", vprev)
                    b.inc(t.transpose(ps_m[:, 0:16],
                                      A_t[0:16, 1024 + k * 128:1024 + (k + 1) * 128],
                                      ident[0:16, 0:16]), "t", 1)
                block.tensor(_)
                tv = b.cnt["t"]

                def _(v, k=k, tv=tv):
                    b.wait(v, "t", tv)
                    b.inc(v.tensor_copy(vT_s[:, k * 16:(k + 1) * 16], ps_m[:, 0:16]),
                          "v", 1)
                block.vector(_)
        setup_marks["v"] = b.cnt["v"]
        setup_marks["t"] = b.cnt["t"]
        if phases < 0:
            if tap == "setup":
                emit_tap(pcol[:])
            return nc

        with nc.sbuf_tensor("ph0", [128, 5376], F32) as P0, \
             nc.psum_tensor("ps_tr0", [128, 1024], F32) as ps_tr:
            xt = [P0[:, 0:1024], P0[:, 1024:2048]]
            zt = [P0[:, 2048:3072], P0[:, 3072:4096]]
            sqs = P0[:, 4096:5120]
            st = P0[:, 5120:5376]
            xsum, xsq = st[:, 0:16], st[:, 16:32]
            mu16, rstd16 = st[:, 32:48], st[:, 48:64]
            nmr16, var16 = st[:, 64:80], st[:, 80:96]
            done = [(0, 0)] * NT
            for si in range(NT):
                xb = xt[si % 2]
                wv_, ws_ = done[si - 2] if si >= 2 else (0, 0)

                def _(sy, si=si, xb=xb, wv_=wv_, ws_=ws_):
                    b.wait(sy, "v", wv_)
                    b.wait(sy, "s", ws_)
                    b.dma(sy, "ld", out=xb, in_=x_d[si * 128:(si + 1) * 128, :])
                block.sync(_)
                ldv = b.cnt["ld"]

                def _(v, si=si, xb=xb, ldv=ldv):
                    b.wait(v, "ld", ldv)
                    b.inc(v.reduce_sum(xsum[:, si:si + 1], xb, axis=AX.X), "v", 1)
                block.vector(_)

                def _(sc, si=si, xb=xb, ldv=ldv):
                    b.wait(sc, "ld", ldv)
                    b.wait(sc, "s")
                    b.inc(sc.activation(sqs, xb, AF.Square,
                                        accum_out=xsq[:, si:si + 1]), "s", 1)
                block.scalar(_)
                done[si] = (b.cnt["v"], b.cnt["s"])

            sstat = b.cnt["s"]

            def _(v, sstat=sstat):
                b.wait(v, "s", sstat)
                b.wait(v, "v")
                b.inc(v.tensor_scalar(mu16, xsum, INV_D, None, ALU.mult), "v", 1)
                b.wait(v, "v")
                b.inc(v.tensor_mul(var16, mu16, mu16), "v", 1)
                b.wait(v, "v")
                b.inc(v.scalar_tensor_tensor(var16, xsq, INV_D, var16,
                                             ALU.mult, ALU.subtract), "v", 1)
            block.vector(_)
            vv = b.cnt["v"]

            def _(sc, vv=vv):
                b.wait(sc, "v", vv)
                b.wait(sc, "s")
                b.inc(sc.activation(rstd16, var16, AF.Ln, bias=epsc[:, 0:1]), "s", 1)
                b.wait(sc, "s")
                b.inc(sc.activation(rstd16, rstd16, AF.Exp, scale=-0.5), "s", 1)
            block.scalar(_)
            sv = b.cnt["s"]

            def _(v, sv=sv):
                b.wait(v, "s", sv)
                b.inc(v.tensor_mul(var16, mu16, rstd16), "v", 1)
                b.wait(v, "v")
                b.inc(v.tensor_scalar(nmr16, var16, -1.0, None, ALU.mult), "v", 1)
            block.vector(_)
            v_st0 = b.cnt["v"]

            t_tr = [0, 0]
            v_ev = 0
            for si in range(NT):
                xb, zb = xt[si % 2], zt[si % 2]
                wv_ = v_ev if si >= 2 else 0

                def _(sy, si=si, xb=xb, wv_=wv_):
                    b.wait(sy, "v", max(wv_, v_st0))
                    b.dma(sy, "ld", out=xb, in_=x_d[si * 128:(si + 1) * 128, :])
                block.sync(_)
                ldv = b.cnt["ld"]
                wtr = t_tr[si % 2]

                def _(v, si=si, xb=xb, zb=zb, ldv=ldv, wtr=wtr):
                    b.wait(v, "ld", ldv)
                    b.wait(v, "t", wtr)
                    b.inc(v.tensor_scalar(zb, xb, rstd16[:, si:si + 1],
                                          nmr16[:, si:si + 1], ALU.mult, ALU.add),
                          "v", 1)
                block.vector(_)
                vz = b.cnt["v"]

                def _(t, si=si, zb=zb, vz=vz, v_ev=v_ev):
                    b.wait(t, "v", vz)
                    if si > 0:
                        b.wait(t, "v", v_ev)
                    last = None
                    for c in range(ND):
                        last = t.transpose(ps_tr[:, c * 128:(c + 1) * 128],
                                           zb[:, c * 128:(c + 1) * 128], ident[:])
                    b.inc(last, "t", 1)
                block.tensor(_)
                t_tr[si % 2] = b.cnt["t"]
                tv = b.cnt["t"]

                def _(v, si=si, tv=tv):
                    b.wait(v, "t", tv)
                    dst = A_t[:].rearrange("p (c t) -> p c t", c=ND)[:, :, si * 128:(si + 1) * 128]
                    b.inc(v.tensor_copy(dst, ps_tr[:].rearrange("p (c f) -> p c f", c=ND)),
                          "v", 1)
                block.vector(_)
                v_ev = b.cnt["v"]
        v_z0 = b.cnt["v"]
        if phases < 1:
            if tap == "z0T":
                emit_tap(A_t[:])
            return nc

        _ph1s_box = {}

        def _ph1s(name):
            return _ph1s_box[name]

        def elu_norm(blk):
            e1, e2, rst = _ph1s("e1"), _ph1s("e2"), _ph1s("rst")
            s_entry = b.cnt["s"]

            def _(v, s_entry=s_entry):
                b.wait(v, "s", s_entry)
                b.wait(v, "v")
                b.inc(v.tensor_scalar(e1, blk, 0.0, None, ALU.min), "v", 1)
            block.vector(_)
            vv = b.cnt["v"]

            def _(sc, vv=vv):
                b.wait(sc, "v", vv)
                b.inc(sc.activation(e2, e1, AF.Exp), "s", 1)
            block.scalar(_)
            sv = b.cnt["s"]

            def _(v, sv=sv):
                b.wait(v, "s", sv)
                b.inc(v.tensor_scalar(e1, blk, 0.0, None, ALU.max), "v", 1)
                b.wait(v, "v")
                b.inc(v.scalar_tensor_tensor(blk, e2, -1.0, e1, ALU.add, ALU.add),
                      "v", 1)
            block.vector(_)
            vv = b.cnt["v"]

            def _(sc, vv=vv):
                b.wait(sc, "v", vv)
                b.inc(sc.activation(e2, blk, AF.Square), "s", 1)
            block.scalar(_)
            sv = b.cnt["s"]

            def _(v, sv=sv):
                b.wait(v, "s", sv)
                b.inc(v.reduce_sum(rst[:, 0:16],
                                   e2.rearrange("p (h d) -> p h d", d=HD), axis=AX.X),
                      "v", 1)
            block.vector(_)
            vv = b.cnt["v"]

            def _(sc, vv=vv):
                b.wait(sc, "v", vv)
                b.inc(sc.activation(rst[:, 16:32], rst[:, 0:16], AF.Ln), "s", 1)
                b.wait(sc, "s")
                b.inc(sc.activation(rst[:, 16:32], rst[:, 16:32], AF.Exp, scale=-0.5),
                      "s", 1)
            block.scalar(_)
            sv = b.cnt["s"]

            def _(v, sv=sv):
                b.wait(v, "s", sv)
                rb = rst[:, 16:32].unsqueeze(2).broadcast_to([128, 16, HD])
                b.inc(v.tensor_tensor(blk.rearrange("p (h d) -> p h d", d=HD),
                                      blk.rearrange("p (h d) -> p h d", d=HD),
                                      rb, ALU.mult), "v", 1)
            block.vector(_)
            return b.cnt["v"]

        def load_resident_w_pairs(dst_h, dst_l, dram, wtmp, wait_v=0):
            s_done = [b.cnt["s"]] * 2
            v_done = [b.cnt["v"]] * 2
            for k in range(ND):
                tb = wtmp[:, (k % 2) * 1024:(k % 2) * 1024 + 1024]
                wv_, ws_ = v_done[k % 2], s_done[k % 2]

                def _(sy, k=k, tb=tb, wv_=wv_, ws_=ws_, wait_v=wait_v):
                    b.wait(sy, "v", max(wv_, wait_v))
                    b.wait(sy, "s", ws_)
                    b.dma(sy, "ld", out=tb, in_=dram[k * 128:(k + 1) * 128, :])
                block.sync(_)
                ldv = b.cnt["ld"]
                hs = dst_h.rearrange("p (k w) -> p k w", k=ND)[:, k, :]
                ls = dst_l.rearrange("p (k w) -> p k w", k=ND)[:, k, :]

                def _(v, tb=tb, hs=hs, ls=ls, ldv=ldv):
                    b.wait(v, "ld", ldv)
                    b.wait(v, "v")
                    b.inc(v.tensor_copy(hs, tb), "v", 1)
                    b.wait(v, "v")
                    b.inc(v.tensor_sub(ls, tb, hs), "v", 1)
                block.vector(_)
                v_done[k % 2] = b.cnt["v"]
                s_done[k % 2] = b.cnt["s"]
            return b.cnt["v"]

        def derive_pair_blocks(zh, zl, si, v_ready, t_free):
            srcs = A_t[:].rearrange("p (c t) -> p c t", c=ND)[:, :, si * 128:(si + 1) * 128]
            zh3 = zh.rearrange("p (c t) -> p c t", c=ND)
            zl3 = zl.rearrange("p (c t) -> p c t", c=ND)

            def _(v, v_ready=v_ready, t_free=t_free):
                b.wait(v, "v", v_ready)
                b.wait(v, "v")
                b.wait(v, "t", t_free)
                b.inc(v.tensor_copy(zh3, srcs), "v", 1)
                b.wait(v, "v")
                b.inc(v.tensor_sub(zl3, srcs, zh3), "v", 1)
            block.vector(_)
            return b.cnt["s"], b.cnt["v"]

        def tok_proj3(t, ps, wh, wl, zh, zl):
            last = None
            for j in range(2):
                cols = slice(j * 512, (j + 1) * 512)
                for pi, (zz, ww) in enumerate(((zh, wh), (zl, wh), (zh, wl))):
                    for k in range(ND):
                        last = t.matmul(
                            ps[:, cols],
                            zz[:, k * 128:(k + 1) * 128],
                            ww.rearrange("p (k w) -> p k w", k=ND)[:, k, cols],
                            start=(pi == 0 and k == 0),
                            stop=(pi == 2 and k == ND - 1))
            return last

        with nc.sbuf_tensor("ph1zp16", [128, 4096], BF16) as ZP16:
            bkb_cm = nc.sbuf_tensor("ph1bias", [128, 1024], F32)
            BKB = bkb_cm.__enter__()
            wp_cm = nc.sbuf_tensor("phwp", [128, 2048], F32)
            WP_scope = wp_cm.__enter__()
            wpb_cm = nc.sbuf_tensor("phwpb", [128, 4096], BF16)
            WPB_scope = wpb_cm.__enter__()
            ph1s_cm = nc.sbuf_tensor("ph1s", [128, 7296], F32)
            PH1S = ph1s_cm.__enter__()
            kbp_ = [PH1S[:, 0:1024], PH1S[:, 2048:3072]]
            vbp_ = [PH1S[:, 1024:2048], PH1S[:, 3072:4096]]
            _ph1s_box.update(e1=PH1S[:, 4096:5120], e2=PH1S[:, 6272:7296],
                             rst=PH1S[:, 5120:5248])
            bvb = PH1S[:, 5248:6272]
            wv_h, wv_l = B16[:, 0:8192], B16[:, 8192:16384]
            wk_h, wk_l = B16[:, 16384:24576], B16[:, 24576:32768]
            zpp = [(ZP16[:, 0:1024], ZP16[:, 1024:2048]),
                   (ZP16[:, 2048:3072], ZP16[:, 3072:4096])]
            bkb = BKB[:]
            wtmp_kv = PH1S[:, 0:2048]

            def _(sy):
                b.wait(sy, "v", v_z0)
                b.dma(sy, "ld", out=bvb,
                                   in_=bias_d["bv"].unsqueeze(0).partition_broadcast(128))
                b.dma(sy, "ld", out=bkb,
                                   in_=bias_d["bk"].unsqueeze(0).partition_broadcast(128))
            block.sync(_)
            ld_b = b.cnt["ld"]
            def _(sy):
                b.wait(sy, "t", setup_marks["t"])
                b.wait(sy, "v", setup_marks["v"])
            block.sync(_)
            v_wv = load_resident_w_pairs(wv_h, wv_l, wv_d, wtmp_kv)
            v_wk = load_resident_w_pairs(wk_h, wk_l, wk_d, wtmp_kv)
            WP = WP_scope
            wp_tmp = [WP[:, 0:1024], WP[:, 1024:2048]]
            wp_bp = [WPB_scope[:, 0:2048], WPB_scope[:, 2048:4096]]
            v_wp = [b.cnt["v"], b.cnt["v"]]
            s_wp = [b.cnt["s"], b.cnt["s"]]
            t_wp0 = b.cnt["t"]
            g_wp = [0, 0]

            def emit_prep_stripe(idx):
                w2_mode = idx >= NF
                i = idx % NF
                par = idx % 2
                tb = wp_tmp[par]
                wpb = wp_bp[par]
                src_ap = (w2_d[i * 128:(i + 1) * 128, :] if w2_mode
                          else w1t_d[i].transpose([1, 0, 2]))

                def _(sy, tb=tb, src_ap=src_ap, wv_=v_wp[par], ws_=s_wp[par],
                      idx=idx):
                    b.wait(sy, "v", wv_)
                    b.wait(sy, "s", ws_)
                    if idx < 2:
                        b.wait(sy, "t", t_wp0)
                    ob = tb
                    if src_ap.ndim == 3:
                        ob = tb.rearrange("p (k c) -> p k c", k=src_ap.shape[1])
                    b.dma(sy, "ld", out=ob, in_=src_ap)
                block.sync(_)
                ldv = b.cnt["ld"]

                def _(v, tb=tb, ldv=ldv, w2_mode=w2_mode):
                    b.wait(v, "ld", ldv)
                    b.wait(v, "v")
                    kap = -KAPPA2 if w2_mode else KAPPA
                    b.inc(v.scalar_tensor_tensor(tb, tb, kap, tb,
                                                 ALU.mult, ALU.add), "v", 1)
                block.vector(_)
                vk = b.cnt["v"]

                def _(sc, tb=tb, wpb=wpb, vk=vk, gw=g_wp[par]):
                    b.wait(sc, "v", vk)
                    b.wait(sc, "gq", gw)
                    b.wait(sc, "s")
                    b.inc(sc.copy(wpb[:, 0:1024], tb), "s", 1)
                block.scalar(_)
                sv = b.cnt["s"]
                s_wp[par] = sv

                def _(v, tb=tb, wpb=wpb, sv=sv):
                    b.wait(v, "s", sv)
                    b.inc(v.tensor_sub(wpb[:, 1024:2048], tb, wpb[:, 0:1024]),
                          "v", 1)
                block.vector(_)
                vv = b.cnt["v"]
                v_wp[par] = vv

                def _(gp, i=i, w2_mode=w2_mode, wpb=wpb, vv=vv):
                    b.wait(gp, "v", vv)
                    for hl in range(2):
                        srcv = wpb[:, hl * 1024:(hl + 1) * 1024]
                        if w2_mode:
                            dst = w2p_d[hl, i]
                        else:
                            dst = w1p_d[hl, i].transpose([1, 0, 2])
                            srcv = srcv.rearrange("p (k c) -> p k c", k=ND)
                        b.dma(gp, "gq", out=dst, in_=srcv)
                block.gpsimd(_)
                g_wp[par] = b.cnt["gq"]


            with nc.psum_tensor("ps_v1", [128, 1024], F32) as ps_v, \
                 nc.psum_tensor("ps_k1", [128, 1024], F32) as ps_k, \
                 nc.psum_tensor("ps_kv", [128, 512], F32) as ps_kv:
                v_evac_last = 0
                t_kv = [0, 0]
                t_proj = [0, 0]
                v_acc = 0
                v_elu = [0] * NT
                dv = [None] * NT
                dv[0] = derive_pair_blocks(zpp[0][0], zpp[0][1], 0, v_z0, 0)

                def emit_kv(si):
                    nonlocal v_acc

                    def _(t, si=si, v_kn=v_elu[si], v_acc_=v_acc):
                        b.wait(t, "v", v_kn)
                        if si >= 1:
                            b.wait(t, "v", v_acc_)
                        kb, vb = kbp_[si % 2], vbp_[si % 2]
                        last = None
                        for h in range(NH):
                            lo = 64 * (h % 2)
                            last = t.matmul(
                                ps_kv[lo:lo + 64, (h // 2) * 64:(h // 2) * 64 + 64],
                                kb[:, h * 64:(h + 1) * 64],
                                vb[:, h * 64:(h + 1) * 64],
                                start=True, stop=True,
                                tile_position=(0, lo))
                        b.inc(last, "t", 1)
                    block.tensor(_)
                    t_kv[si % 2] = b.cnt["t"]

                    def _(v, si=si, tkv=t_kv[si % 2]):
                        b.wait(v, "t", tkv)
                        b.wait(v, "v")
                        if si == 0:
                            b.inc(v.tensor_copy(kv_s[:], ps_kv[:]), "v", 1)
                        else:
                            b.inc(v.tensor_add(kv_s[:], kv_s[:], ps_kv[:]), "v", 1)
                    block.vector(_)
                    v_acc = b.cnt["v"]

                for si in range(NT):
                    zh, zl = zpp[si % 2]

                    def _(t, si=si, v_evac_last=v_evac_last, spvp=dv[si]):
                        b.wait(t, "v", max(v_wk, spvp[1]))
                        if si >= 1:
                            b.wait(t, "v", v_evac_last)
                        tok_proj3(t, ps_v, wv_h, wv_l, zh, zl)
                        b.inc(tok_proj3(t, ps_k, wk_h, wk_l, zh, zl), "t", 1)
                    block.tensor(_)
                    t_proj[si % 2] = b.cnt["t"]
                    tp = b.cnt["t"]

                    if si >= 1:
                        emit_kv(si - 1)

                    def _(v, si=si, tp=tp, tkv=t_kv[si % 2]):
                        b.wait(v, "t", tp)
                        b.wait(v, "ld", ld_b)
                        b.wait(v, "t", tkv)
                        v.tensor_add(vbp_[si % 2], ps_v[:], bvb)
                        b.inc(v.tensor_add(kbp_[si % 2], ps_k[:], bkb), "v", 1)
                    block.vector(_)
                    v_evac_last = b.cnt["v"]
                    if si + 1 < NT:
                        dv[si + 1] = derive_pair_blocks(
                            zpp[(si + 1) % 2][0], zpp[(si + 1) % 2][1], si + 1,
                            v_z0, t_proj[(si + 1) % 2])
                    v_elu[si] = elu_norm(kbp_[si % 2])
                    for pj in range(4):
                        emit_prep_stripe(si * 4 + pj)
                emit_kv(NT - 1)
                g_wprep = b.cnt["gq"]

                def _(v):
                    b.wait(v, "v")
                    b.inc(v.tensor_scalar(kv_s[:], kv_s[:], 0.125, None, ALU.mult),
                          "v", 1)
                block.vector(_)
                def _(v):
                    b.wait(v, "v")
                    b.inc(v.tensor_copy(kvp_h, kv_s[:]), "v", 1)
                    b.wait(v, "v")
                    b.inc(v.tensor_sub(kvp_l, kv_s[:], kvp_h), "v", 1)
                block.vector(_)
            v_kv = b.cnt["v"]
            ph1s_cm.__exit__(None, None, None)
            wpb_cm.__exit__(None, None, None)
            wp_cm.__exit__(None, None, None)
            bkb_cm.__exit__(None, None, None)
            if phases < 2:
                if tap == "kv":
                    emit_tap(kv_s[:])
                return nc

            with nc.sbuf_tensor("ph1q", [128, 5376], F32) as PQ, \
                 nc.sbuf_tensor("ph1qw16", [128, 16384], BF16) as W16, \
                 nc.sbuf_tensor("ph1qp16", [128, 2048], BF16) as QP16:
                wq_h, wq_l = W16[:, 0:8192], W16[:, 8192:16384]
                qbp = [PQ[:, 0:1024], PQ[:, 4352:5376]]
                q1 = PQ[:, 1024:2048]
                q2 = PQ[:, 2048:3072]
                qrst = PQ[:, 3072:3200]
                bq_pq = PQ[:, 3328:4352]

                def elu_norm_q(blk):
                    s_entry = b.cnt["s"]

                    def _(v, s_entry=s_entry):
                        b.wait(v, "s", s_entry)
                        b.wait(v, "v")
                        b.inc(v.tensor_scalar(q1, blk, 0.0, None, ALU.min), "v", 1)
                    block.vector(_)
                    vv = b.cnt["v"]

                    def _(sc, vv=vv):
                        b.wait(sc, "v", vv)
                        b.inc(sc.activation(q2, q1, AF.Exp), "s", 1)
                    block.scalar(_)
                    sv = b.cnt["s"]

                    def _(v, sv=sv):
                        b.wait(v, "s", sv)
                        b.inc(v.tensor_scalar(q1, blk, 0.0, None, ALU.max), "v", 1)
                        b.wait(v, "v")
                        b.inc(v.scalar_tensor_tensor(blk, q2, -1.0, q1,
                                                     ALU.add, ALU.add), "v", 1)
                    block.vector(_)
                    vv = b.cnt["v"]

                    def _(sc, vv=vv):
                        b.wait(sc, "v", vv)
                        b.inc(sc.activation(q2, blk, AF.Square), "s", 1)
                    block.scalar(_)
                    sv = b.cnt["s"]

                    def _(v, sv=sv):
                        b.wait(v, "s", sv)
                        b.inc(v.reduce_sum(qrst[:, 0:16],
                                           q2.rearrange("p (h d) -> p h d", d=HD),
                                           axis=AX.X), "v", 1)
                    block.vector(_)
                    vv = b.cnt["v"]

                    def _(sc, vv=vv):
                        b.wait(sc, "v", vv)
                        b.inc(sc.activation(qrst[:, 16:32], qrst[:, 0:16], AF.Ln),
                              "s", 1)
                        b.wait(sc, "s")
                        b.inc(sc.activation(qrst[:, 16:32], qrst[:, 16:32], AF.Exp,
                                            scale=-0.5), "s", 1)
                    block.scalar(_)
                    sv = b.cnt["s"]

                    def _(v, sv=sv):
                        b.wait(v, "s", sv)
                        rb = qrst[:, 16:32].unsqueeze(2).broadcast_to([128, 16, HD])
                        b.inc(v.tensor_tensor(blk.rearrange("p (h d) -> p h d", d=HD),
                                              blk.rearrange("p (h d) -> p h d", d=HD),
                                              rb, ALU.mult), "v", 1)
                    block.vector(_)
                    return b.cnt["v"]

                bq_tile = bq_pq

                def _(sy, v_kv=v_kv):
                    b.wait(sy, "v", v_kv)
                    b.wait(sy, "gq", g_wprep)
                    b.dma(sy, "ld", out=bq_tile,
                          in_=bias_d["bq"].unsqueeze(0).partition_broadcast(128))
                block.sync(_)
                ld_bq = b.cnt["ld"]
                wtmp_q = PQ[:, 1024:3072]
                v_wq = load_resident_w_pairs(wq_h, wq_l, wq_d, wtmp_q, wait_v=v_kv)
                qp16 = QP16[:]

                with nc.psum_tensor("ps_q1", [128, 1024], F32) as ps_q, \
                     nc.psum_tensor("ps_trq16", [128, 2048], BF16) as ps_tq:
                    v_ev = 0
                    v_bias = 0
                    t_tr2 = [0, 0]
                    v_qn = [0] * NT

                    def emit_qtr(si):
                        nonlocal v_ev

                        def _(v, si=si, vq=v_qn[si]):
                            b.wait(v, "v", vq)
                            b.wait(v, "v")
                            qb = qbp[si % 2]
                            b.inc(v.tensor_copy(qp16[:, 0:1024], qb), "v", 1)
                            b.wait(v, "v")
                            b.inc(v.tensor_sub(qp16[:, 1024:2048], qb,
                                               qp16[:, 0:1024]), "v", 1)
                        block.vector(_)
                        vqp = b.cnt["v"]

                        def _(t, si=si, vqp=vqp, v_ev_=v_ev):
                            b.wait(t, "v", vqp)
                            if si > 0:
                                b.wait(t, "v", v_ev_)
                            last = None
                            for c in range(ND):
                                last = t.transpose(ps_tq[:, c * 128:(c + 1) * 128],
                                                   qp16[:, c * 128:(c + 1) * 128],
                                                   ident_b[:])
                            for c in range(ND):
                                last = t.transpose(
                                    ps_tq[:, 1024 + c * 128:1024 + (c + 1) * 128],
                                    qp16[:, 1024 + c * 128:1024 + (c + 1) * 128],
                                    ident_b[:])
                            b.inc(last, "t", 1)
                        block.tensor(_)
                        t_tr2[si % 2] = b.cnt["t"]
                        tv = b.cnt["t"]

                        def _(v, si=si, tv=tv):
                            b.wait(v, "t", tv)
                            dsth = B16[:].rearrange("p (c t) -> p c t", c=ND)[
                                :, :, si * 128:(si + 1) * 128]
                            b.inc(v.tensor_copy(
                                dsth, ps_tq[:, 0:1024].rearrange("p (c f) -> p c f", c=ND)),
                                "v", 1)
                            b.wait(v, "v")
                            dstl = B16[:].rearrange("p (c t) -> p c t", c=ND)[
                                :, :, S + si * 128:S + (si + 1) * 128]
                            b.inc(v.tensor_copy(
                                dstl, ps_tq[:, 1024:2048].rearrange("p (c f) -> p c f", c=ND)),
                                "v", 1)
                        block.vector(_)
                        v_ev = b.cnt["v"]

                    t_projq = [0, 0]
                    dvq = [None] * NT
                    dvq[0] = derive_pair_blocks(zpp[0][0], zpp[0][1], 0, v_kv, 0)
                    for si in range(NT):
                        zh, zl = zpp[si % 2]

                        def _(t, si=si, v_bias=v_bias, spvp=dvq[si]):
                            b.wait(t, "v", max(v_wq, spvp[1]))
                            b.wait(t, "ld", ld_bq)
                            if si >= 1:
                                b.wait(t, "v", v_bias)
                            b.inc(tok_proj3(t, ps_q, wq_h, wq_l, zh, zl), "t", 1)
                        block.tensor(_)
                        t_projq[si % 2] = b.cnt["t"]
                        tp = b.cnt["t"]

                        if si >= 1:
                            emit_qtr(si - 1)

                        def _(v, si=si, tp=tp, wtr=t_tr2[si % 2]):
                            b.wait(v, "t", tp)
                            b.wait(v, "t", wtr)
                            b.inc(v.tensor_add(qbp[si % 2], ps_q[:], bq_tile), "v", 1)
                        block.vector(_)
                        v_bias = b.cnt["v"]
                        if si + 1 < NT:
                            dvq[si + 1] = derive_pair_blocks(
                                zpp[(si + 1) % 2][0], zpp[(si + 1) % 2][1], si + 1,
                                v_kv, t_projq[(si + 1) % 2])
                        v_qn[si] = elu_norm_q(qbp[si % 2])
                    emit_qtr(NT - 1)
        v_qnT = b.cnt["v"]
        if phases < 3:
            if tap == "qnT":
                emit_tap(B16[:, 0:ND * S])
            return nc

        with nc.psum_tensor("ps_ctx", [128, S], F32) as ps_ctx:
            v_ev = 0
            for c in range(ND):
                def _(t, c=c, v_ev=v_ev):
                    b.wait(t, "v", v_qnT)
                    if c > 0:
                        b.wait(t, "v", v_ev)
                    last = None
                    for par in range(2):
                        h = 2 * c + par
                        lo = 64 * par
                        kc = slice((h // 2) * 64, (h // 2) * 64 + 64)
                        for j in range(4):
                            cols = slice(j * 512, (j + 1) * 512)
                            for pi, (kk, qq) in enumerate(
                                    ((kvp_h, Bp_hc(c)), (kvp_l, Bp_hc(c)),
                                     (kvp_h, Bp_lc(c)))):
                                last = t.matmul(
                                    ps_ctx[lo:lo + 64, cols],
                                    kk[lo:lo + 64, kc],
                                    qq[lo:lo + 64, cols],
                                    start=(pi == 0), stop=(pi == 2),
                                    tile_position=(lo, lo))
                    b.inc(last, "t", 1)
                block.tensor(_)
                tv = b.cnt["t"]

                def _(v, c=c, tv=tv):
                    b.wait(v, "t", tv)
                    b.wait(v, "v")
                    b.inc(v.tensor_copy(Bp_hc(c), ps_ctx[:]), "v", 1)
                    b.wait(v, "v")
                    b.inc(v.tensor_sub(Bp_lc(c), ps_ctx[:], Bp_hc(c)), "v", 1)
                block.vector(_)
                v_ev = b.cnt["v"]
        v_ctx = b.cnt["v"]
        if phases < 4:
            if tap == "ctxT":
                emit_tap(B16[:, 0:ND * S])
            return nc

        with nc.sbuf_tensor("ph3w16", [128, 16384], BF16) as WO16, \
             nc.sbuf_tensor("ph3wt", [128, 2048], F32) as WOT:
            wo_h, wo_l = WO16[:, 0:8192], WO16[:, 8192:16384]
            wtmp_o = WOT[:]
            v_wo = load_resident_w_pairs(wo_h, wo_l, wo_d, wtmp_o, wait_v=v_ctx)
            Bph = Bp_hc
            Bpl = Bp_lc

            with nc.psum_tensor("ps_wo", [128, S], F32) as ps_wo:
                v_ev = 0
                for n in range(ND):
                    def _(t, n=n, v_ev=v_ev):
                        b.wait(t, "v", max(v_wo, v_ctx))
                        if n > 0:
                            b.wait(t, "v", v_ev)
                        last = None
                        woh3 = wo_h.rearrange("p (k w) -> p k w", k=ND)
                        wol3 = wo_l.rearrange("p (k w) -> p k w", k=ND)
                        for j in range(4):
                            cols = slice(j * 512, (j + 1) * 512)
                            for pi, (wfn, bfn) in enumerate(
                                    ((woh3, Bph), (woh3, Bpl), (wol3, Bph))):
                                for e_ in range(ND):
                                    t.matmul(ps_wo[:, cols],
                                             wfn[:, e_, n * 128:(n + 1) * 128],
                                             bfn(e_)[:, cols],
                                             start=(pi == 0 and e_ == 0), stop=False)
                            last = t.matmul(ps_wo[:, cols], ident[:],
                                            A_c(n)[:, cols], start=False, stop=True)
                        b.inc(last, "t", 1)
                    block.tensor(_)
                    tv = b.cnt["t"]

                    def _(v, n=n, tv=tv):
                        b.wait(v, "t", tv)
                        b.inc(v.tensor_scalar(A_c(n), ps_wo[:], pc("bo", n), None,
                                              ALU.add), "v", 1)
                    block.vector(_)
                    v_ev = b.cnt["v"]
        v_pre1 = b.cnt["v"]
        if phases < 5:
            if tap == "pre1T":
                emit_tap(A_t[:])
            return nc

        def ln_stats_full(eps, v_ready, mu_row, rs_row, bc_mu, bc_rs, sq):
            with nc.psum_tensor(f"ps_s1_{b.cnt['t']}", [1, S], F32) as ps1, \
                 nc.psum_tensor(f"ps_s2_{b.cnt['t']}", [1, S], F32) as ps2:
                for c in range(ND):
                    tprev = b.cnt["t"]

                    def _(sc, c=c, tprev=tprev):
                        b.wait(sc, "v", v_ready)
                        if c >= 1:
                            b.wait(sc, "t", tprev)
                        b.inc(sc.activation(sq, A_c(c), AF.Square), "s", 1)
                    block.scalar(_)
                    sv = b.cnt["s"]

                    def _(t, c=c, sv=sv):
                        b.wait(t, "v", v_ready)
                        b.wait(t, "s", sv)
                        last = None
                        for j in range(4):
                            cols = slice(j * 512, (j + 1) * 512)
                            t.matmul(ps1[:, cols], ones_c[:], A_c(c)[:, cols],
                                     start=(c == 0), stop=(c == ND - 1))
                            last = t.matmul(ps2[:, cols], ones_c[:], sq[:, cols],
                                            start=(c == 0), stop=(c == ND - 1))
                        b.inc(last, "t", 1)
                    block.tensor(_)
                tst = b.cnt["t"]

                def _(v, tst=tst):
                    b.wait(v, "t", tst)
                    b.wait(v, "v")
                    b.inc(v.tensor_scalar(mu_row, ps1[:], INV_D, None, ALU.mult), "v", 1)
                    b.wait(v, "v")
                    b.inc(v.tensor_mul(rs_row, mu_row, mu_row), "v", 1)
                    b.wait(v, "v")
                    b.inc(v.scalar_tensor_tensor(rs_row, ps2[:], INV_D, rs_row,
                                                 ALU.mult, ALU.subtract), "v", 1)
                block.vector(_)
                vv = b.cnt["v"]

                def _(sc, vv=vv, eps=eps):
                    b.wait(sc, "v", vv)
                    bp = rs_row.base_partition()
                    b.wait(sc, "s")
                    b.inc(sc.activation(rs_row, rs_row, AF.Ln,
                                        bias=eps[bp:bp + 1, 0:1]), "s", 1)
                    b.wait(sc, "s")
                    b.inc(sc.activation(rs_row, rs_row, AF.Exp, scale=-0.5), "s", 1)
                block.scalar(_)
                v_rows = b.cnt["v"]

                def _(v):
                    b.wait(v, "s")
                    b.inc(v.engine_nop(), "v", 1)
                block.vector(_)
                v_rows = b.cnt["v"]

            if bc_mu is None:
                return v_rows
            with nc.psum_tensor(f"ps_bc_{b.cnt['t']}", [128, S], F32) as ps_bc:
                for src_row, dst_bc in [(mu_row, bc_mu), (rs_row, bc_rs)]:
                    vw = b.cnt["v"]
                    bp = src_row.base_partition()
                    one1 = ones_r[bp:bp + 1, :]

                    def _(t, src_row=src_row, vw=vw, v_rows=v_rows, one1=one1):
                        b.wait(t, "v", max(v_rows, vw))
                        last = None
                        for j in range(4):
                            cols = slice(j * 512, (j + 1) * 512)
                            last = t.matmul(ps_bc[:, cols], one1,
                                            src_row[:, cols], start=True, stop=True)
                        b.inc(last, "t", 1)
                    block.tensor(_)
                    tv = b.cnt["t"]

                    def _(v, dst_bc=dst_bc, tv=tv):
                        b.wait(v, "t", tv)
                        b.inc(v.tensor_copy(dst_bc, ps_bc[:]), "v", 1)
                    block.vector(_)
            return b.cnt["v"]

        with nc.sbuf_tensor("ph4", [128, 8192], F32) as P4:
            sq_s = P4[:, 0:2048]
            bc_mu = P4[:, 2048:4096]
            bc_rs = P4[:, 4096:6144]
            p1s = P4[:, 6144:8192]
            v_bc = ln_stats_full(epsc[:, 1:2], v_pre1, rows[0:1, :], rows[32:33, :],
                                 bc_mu, bc_rs, sq_s)
            for c in range(ND):
                def _(v, c=c, v_bc=v_bc):
                    b.wait(v, "v", v_bc)
                    b.wait(v, "v")
                    b.inc(v.tensor_sub(p1s, A_c(c), bc_mu), "v", 1)
                    b.wait(v, "v")
                    b.inc(v.scalar_tensor_tensor(p1s, p1s, pc("g1", c), bc_rs,
                                                 ALU.mult, ALU.mult), "v", 1)
                    b.wait(v, "v")
                    b.inc(v.tensor_scalar(A_c(c), p1s, pc("be1", c), None, ALU.add),
                          "v", 1)
                block.vector(_)
        v_attn = b.cnt["v"]


        if phases < 6:
            if tap == "attn_outT":
                emit_tap(A_t[:])
            return nc

        with nc.sbuf_tensor("ph5", [128, 9728], F32) as P5, \
             nc.sbuf_tensor("ph5p", [128, 12288], BF16) as P5P:
            wpair = [(P5P[:, 0:1024], P5P[:, 1024:2048]),
                     (P5P[:, 2048:3072], P5P[:, 3072:4096])]
            apg_h3 = P5P[:, 4096:8192].rearrange("p (c t) -> p c t", c=ND)
            apg_l3 = P5P[:, 8192:12288].rearrange("p (c t) -> p c t", c=ND)
            xTg = P5[:, 0:4096]
            pre2f = P5[:, 5632:9728]

            v_group_prev = v_attn

            for g in range(G):
                xr = P5[:, 5632:6656]
                with nc.psum_tensor(f"ps_trx_{g}", [128, 1024], F32) as ps_trx:
                    v_ev = 0
                    t_prev = 0
                    for tt in range(NTG):
                        def _(gp, g=g, tt=tt, t_prev=t_prev, v_group_prev=v_group_prev):
                            b.wait(gp, "t", t_prev)
                            b.wait(gp, "v", v_group_prev)
                            r0 = g * GS + tt * 128
                            b.dma(gp, "gq", out=xr, in_=x_d[r0:r0 + 128, :])
                        block.gpsimd(_)
                        gv = b.cnt["gq"]

                        def _(t, gv=gv, v_ev=v_ev):
                            b.wait(t, "gq", gv)
                            if v_ev:
                                b.wait(t, "v", v_ev)
                            last = None
                            for c in range(ND):
                                last = t.transpose(ps_trx[:, c * 128:(c + 1) * 128],
                                                   xr[:, c * 128:(c + 1) * 128],
                                                   ident[:])
                            b.inc(last, "t", 1)
                        block.tensor(_)
                        t_prev = b.cnt["t"]

                        def _(v, tt=tt, t_prev=t_prev):
                            b.wait(v, "t", t_prev)
                            dst = xTg.rearrange("p (c t) -> p c t", c=ND)[:, :, tt * 128:(tt + 1) * 128]
                            b.inc(v.tensor_copy(dst,
                                                ps_trx[:].rearrange("p (c f) -> p c f", c=ND)),
                                  "v", 1)
                        block.vector(_)
                        v_ev = b.cnt["v"]
                v_xT = b.cnt["v"]

                sA = b.cnt["s"]
                tA = b.cnt["t"]
                srcg = A_t[:].rearrange("p (c t) -> p c t", c=ND)[:, :, g * GS:(g + 1) * GS]

                def _(v, g=g, tA=tA, v_group_prev=v_group_prev):
                    b.wait(v, "v", v_attn)
                    b.wait(v, "v", v_group_prev)
                    b.wait(v, "v")
                    b.wait(v, "t", tA)
                    b.wait(v, "gq", g_wprep)
                    b.inc(v.tensor_copy(apg_h3, srcg), "v", 1)
                    b.wait(v, "v")
                    b.inc(v.tensor_sub(apg_l3, srcg, apg_h3), "v", 1)
                block.vector(_)
                v_apg = b.cnt["v"]
                s_apg = b.cnt["s"]

                with nc.psum_tensor(f"ps_h_{g}", [128, 1536], F32) as ps_h:
                    t_sweep0 = b.cnt["t"]
                    s_gelu = [0, 0]
                    v_gelu_last = 0
                    t_w1 = [t_sweep0, t_sweep0]
                    for i in range(NF):
                        def _(sy, i=i, ws_=t_w1[i % 2], wv_=(v_xT if i < 2 else 0),
                              vg_=(v_group_prev if i < 2 else 0)):
                            b.wait(sy, "t", ws_)
                            b.wait(sy, "v", max(wv_, vg_))
                            b.wait(sy, "gq", g_wprep)
                            wh, wl = wpair[i % 2]
                            for hl, dstv in ((0, wh), (1, wl)):
                                b.dma(sy, "ld", 
                                    out=dstv.rearrange("p (k c) -> p k c", k=ND),
                                    in_=w1p_d[hl, i].transpose([1, 0, 2]))
                        block.sync(_)
                        v_w1 = b.cnt["ld"]
                        sg = s_gelu[i % 2]

                        def _(t, g=g, i=i, v_w1=v_w1, sg=sg, v_apg=v_apg,
                              s_apg=s_apg, v_xT=v_xT):
                            b.wait(t, "ld", v_w1)
                            b.wait(t, "v", v_apg)
                            b.wait(t, "s", s_apg)
                            b.wait(t, "s", sg)
                            b.wait(t, "v", v_xT)
                            dst = ps_h[:, (i % 2) * 512:(i % 2) * 512 + 512]
                            wh, wl = wpair[i % 2]
                            last = None
                            for pi, (aa, ww) in enumerate(
                                    ((apg_h3, wh), (apg_l3, wh), (apg_h3, wl))):
                                for k in range(ND):
                                    last = t.matmul(dst, ww[:, k * 128:(k + 1) * 128],
                                                    aa[:, k, :],
                                                    start=(pi == 0 and k == 0),
                                                    stop=(pi == 2 and k == ND - 1))
                            b.inc(last, "t", 1)
                        block.tensor(_)
                        t_w1[i % 2] = b.cnt["t"]
                        tv = b.cnt["t"]

                        def _(sc, i=i, tv=tv, v_group_prev=v_group_prev,
                              vg=v_gelu_last):
                            b.wait(sc, "t", tv)
                            b.wait(sc, "v", v_group_prev)
                            b.wait(sc, "v", vg)
                            b.wait(sc, "s")
                            src = ps_h[:, (i % 2) * 512:(i % 2) * 512 + 512]
                            b.inc(sc.activation(ps_h[:, 1024:1536], src, AF.Gelu,
                                                bias=pc("b1", i)), "s", 1)
                        block.scalar(_)
                        sv2 = b.cnt["s"]

                        def _(v, i=i, sv2=sv2):
                            b.wait(v, "s", sv2)
                            b.wait(v, "v")
                            b.inc(v.tensor_copy(B16[:, i * 1024:i * 1024 + 512],
                                                ps_h[:, 1024:1536]), "v", 1)
                            b.wait(v, "v")
                            b.inc(v.tensor_sub(B16[:, i * 1024 + 512:i * 1024 + 1024],
                                               ps_h[:, 1024:1536],
                                               B16[:, i * 1024:i * 1024 + 512]),
                                  "v", 1)
                        block.vector(_)
                        v_gelu_last = b.cnt["v"]
                        s_gelu[i % 2] = b.cnt["s"]
                s_h = b.cnt["s"]
                v_h = b.cnt["v"]

                with nc.psum_tensor(f"ps_p2_{g}", [128, ND * 512], F32) as ps_p2:
                    t_sw2 = b.cnt["t"]
                    t_w2 = [t_sw2, t_sw2]
                    for i in range(NF):
                        def _(sy, i=i, ws_=t_w2[i % 2], wv_=(v_h if i < 2 else 0)):
                            b.wait(sy, "t", ws_)
                            b.wait(sy, "v", wv_)
                            b.wait(sy, "gq", g_wprep)
                            wh, wl = wpair[i % 2]
                            for hl, dstv in ((0, wh), (1, wl)):
                                b.dma(sy, "ld", out=dstv, in_=w2p_d[hl, i])
                        block.sync(_)
                        v_w2 = b.cnt["ld"]

                        def _(t, g=g, i=i, v_w2=v_w2, s_h=s_h, v_h=v_h):
                            b.wait(t, "ld", v_w2)
                            b.wait(t, "v", v_h)
                            b.wait(t, "s", s_h)
                            wh, wl = wpair[i % 2]
                            hh = B16[:, i * 1024:i * 1024 + 512]
                            hl = B16[:, i * 1024 + 512:i * 1024 + 1024]
                            last = None
                            for n in range(ND):
                                for pi, (mm, ww) in enumerate(
                                        ((hh, wh), (hl, wh), (hh, wl))):
                                    last = t.matmul(
                                        ps_p2[:, n * 512:(n + 1) * 512],
                                        ww[:, n * 128:(n + 1) * 128], mm,
                                        start=(i == 0 and pi == 0), stop=False)
                            if i == NF - 1:
                                for n in range(ND):
                                    last = t.matmul(ps_p2[:, n * 512:(n + 1) * 512],
                                                    ident[:],
                                                    A_c(n)[:, g * GS:(g + 1) * GS],
                                                    start=False, stop=True)
                            b.inc(last, "t", 1)
                        block.tensor(_)
                        t_w2[i % 2] = b.cnt["t"]
                    t_p2 = b.cnt["t"]

                    for n in range(ND):
                        def _(v, n=n, t_p2=t_p2):
                            b.wait(v, "t", t_p2)
                            b.inc(v.tensor_scalar(pre2f[:, n * 512:(n + 1) * 512],
                                                  ps_p2[:, n * 512:(n + 1) * 512],
                                                  pc("b2", n), None, ALU.add), "v", 1)
                        block.vector(_)
                    v_p2 = b.cnt["v"]

                sqg = P5[:, 4096:4608]
                mu2b = P5[:, 4608:5120]
                rs2b = P5[:, 5120:5632]
                r512a = rows[0:1, 0:512]
                r512b = rows[32:33, 0:512]
                with nc.psum_tensor(f"ps_sg1_{g}", [1, 512], F32) as psg1, \
                     nc.psum_tensor(f"ps_sg2_{g}", [1, 512], F32) as psg2:
                    for n in range(ND):
                        tprev = b.cnt["t"]

                        def _(sc, n=n, v_p2=v_p2, tprev=tprev):
                            b.wait(sc, "v", v_p2)
                            if n >= 1:
                                b.wait(sc, "t", tprev)
                            b.inc(sc.activation(sqg, pre2f[:, n * 512:(n + 1) * 512],
                                                AF.Square), "s", 1)
                        block.scalar(_)
                        sv = b.cnt["s"]

                        def _(t, n=n, sv=sv, v_p2=v_p2):
                            b.wait(t, "s", sv)
                            b.wait(t, "v", v_p2)
                            t.matmul(psg1[:], ones_c[:],
                                     pre2f[:, n * 512:(n + 1) * 512],
                                     start=(n == 0), stop=(n == ND - 1))
                            b.inc(t.matmul(psg2[:], ones_c[:], sqg,
                                           start=(n == 0), stop=(n == ND - 1)), "t", 1)
                        block.tensor(_)
                    tst = b.cnt["t"]

                    def _(v, tst=tst):
                        b.wait(v, "t", tst)
                        b.wait(v, "v")
                        b.inc(v.tensor_scalar(r512a, psg1[:], INV_D, None, ALU.mult), "v", 1)
                        b.wait(v, "v")
                        b.inc(v.tensor_mul(r512b, r512a, r512a), "v", 1)
                        b.wait(v, "v")
                        b.inc(v.scalar_tensor_tensor(r512b, psg2[:], INV_D, r512b,
                                                     ALU.mult, ALU.subtract), "v", 1)
                    block.vector(_)
                    vv = b.cnt["v"]

                    def _(sc, vv=vv):
                        b.wait(sc, "v", vv)
                        b.wait(sc, "s")
                        b.inc(sc.activation(r512b, r512b, AF.Ln,
                                            bias=epsc[32:33, 1:2]), "s", 1)
                        b.wait(sc, "s")
                        b.inc(sc.activation(r512b, r512b, AF.Exp, scale=-0.5), "s", 1)
                    block.scalar(_)
                    sv = b.cnt["s"]

                    def _(v, sv=sv):
                        b.wait(v, "s", sv)
                        b.inc(v.engine_nop(), "v", 1)
                    block.vector(_)
                    v_rows2 = b.cnt["v"]

                with nc.psum_tensor(f"ps_bcg_{g}", [128, 1024], F32) as ps_bcg:
                    def _(t, v_rows2=v_rows2):
                        b.wait(t, "v", v_rows2)
                        t.matmul(ps_bcg[:, 0:512], ones_r[0:1, :], r512a,
                                 start=True, stop=True)
                        b.inc(t.matmul(ps_bcg[:, 512:1024], ones_r[32:33, :], r512b,
                                       start=True, stop=True), "t", 1)
                    block.tensor(_)
                    tv = b.cnt["t"]

                    def _(v, tv=tv):
                        b.wait(v, "t", tv)
                        v.tensor_copy(mu2b, ps_bcg[:, 0:512])
                        b.inc(v.tensor_copy(rs2b, ps_bcg[:, 512:1024]), "v", 1)
                    block.vector(_)
                    v_bc2 = b.cnt["v"]

                for n in range(ND):
                    def _(v, n=n, g=g, v_bc2=v_bc2, v_xT=v_xT):
                        b.wait(v, "v", v_bc2)
                        b.wait(v, "v", v_xT)
                        b.wait(v, "v")
                        b.inc(v.tensor_sub(sqg, pre2f[:, n * 512:(n + 1) * 512],
                                           mu2b), "v", 1)
                        b.wait(v, "v")
                        b.inc(v.scalar_tensor_tensor(sqg, sqg, pc("g2", n), rs2b,
                                                     ALU.mult, ALU.mult), "v", 1)
                        b.wait(v, "v")
                        xc = xTg.rearrange("p (c t) -> p c t", c=ND)[:, n, :]
                        b.inc(v.scalar_tensor_tensor(
                            A_c(n)[:, g * GS:(g + 1) * GS], sqg,
                            pc("be2", n), xc, ALU.add, ALU.add), "v", 1)
                    block.vector(_)
                v_group_prev = b.cnt["v"]
        v_x1 = b.cnt["v"]
        if phases < 7:
            if tap == "x1T":
                emit_tap(A_t[:])
            return nc

        with nc.sbuf_tensor("ph6", [128, 2048], F32) as P6:
            ln_stats_full(epsc[:, 0:1], v_x1, rows[0:1, :], rows[32:33, :],
                          None, None, P6[:, 0:2048])
        v_ln0b = b.cnt["v"]

        ug_all = en(nc.sbuf_tensor("ug_all", [128, 512], F32))
        ph7_cm = nc.sbuf_tensor("ph7", [128, 11266], F32)
        P7 = ph7_cm.__enter__()
        ph7b_cm = nc.sbuf_tensor("ph7b", [128, 8192], BF16)
        P7B = ph7b_cm.__enter__()
        ttf4 = P7[:, 0:4096]
        abs4 = P7[:, 4096:8192]
        ab_s = P7[:, 8192:9216]
        wfbuf = [P7[:, 9216:10240], P7[:, 10240:11264]]
        thr_s = P7[:, 11264:11266]
        tern_b = P7B[:, 0:4096]
        evac_fc = [P7B[:, 4096:5120], P7B[:, 5120:6144]]
        evac_pj = [P7B[:, 6144:7168], P7B[:, 7168:8192]]

        with nc.psum_tensor("ps_trf", [128, 1024], BF16) as ps_trf:
            g_fc = [0, 0]
            t_trf = 0
            for si in range(NF):
                wb_ = wfbuf[si % 2]
                vprev = b.cnt["v"]

                def _(sy, si=si, wb_=wb_, vprev=vprev):
                    b.wait(sy, "v", max(vprev, v_ln0b))
                    for hh in range(2):
                        b.dma(sy, "ld", out=wb_[:, hh * 512:(hh + 1) * 512],
                              in_=wfc_d[si * 128:(si + 1) * 128,
                                        hh * 512:(hh + 1) * 512])
                block.sync(_)
                ldv = b.cnt["ld"]

                def _(sc, si=si, wb_=wb_, ldv=ldv):
                    b.wait(sc, "ld", ldv)
                    b.inc(sc.activation(ab_s, wb_, AF.Abs,
                                        accum_out=sc_fc[:, si:si + 1]), "s", 1)
                block.scalar(_)
                sv = b.cnt["s"]

                def _(v, si=si, wb_=wb_, sv=sv, t_trf=t_trf):
                    b.wait(v, "s", sv)
                    b.wait(v, "t", t_trf)
                    b.wait(v, "v")
                    b.inc(v.tensor_scalar(sc_fc[:, si:si + 1], sc_fc[:, si:si + 1],
                                          INV_D, None, ALU.mult), "v", 1)
                    b.wait(v, "v")
                    b.inc(v.tensor_scalar(sc_fc[:, si:si + 1], sc_fc[:, si:si + 1],
                                          1e-5, None, ALU.max), "v", 1)
                    b.wait(v, "v")
                    v.tensor_scalar(thr_s[:, 0:1], sc_fc[:, si:si + 1], 0.5,
                                    None, ALU.mult)
                    b.inc(v.tensor_scalar(thr_s[:, 1:2], sc_fc[:, si:si + 1], -0.5,
                                          None, ALU.mult), "v", 1)
                    b.wait(v, "v")
                    b.inc(v.tensor_scalar(ab_s, wb_, thr_s[:, 0:1], None, ALU.is_gt),
                          "v", 1)
                    b.wait(v, "v")
                    b.inc(v.scalar_tensor_tensor(
                        tern_b[:, 0:1024],
                        wb_, thr_s[:, 1:2], ab_s, ALU.is_lt, ALU.subtract), "v", 1)
                block.vector(_)
                vv = b.cnt["v"]

                def _(t, vv=vv):
                    b.wait(t, "v", vv)
                    last = None
                    for k in range(ND):
                        last = t.transpose(ps_trf[:, k * 128:(k + 1) * 128],
                                           tern_b[:, k * 128:(k + 1) * 128],
                                           ident_b[:])
                    b.inc(last, "t", 1)
                block.tensor(_)
                t_trf = b.cnt["t"]

                ev_ = evac_fc[si % 2]

                def _(v, t_trf=t_trf, gw=g_fc[si % 2], ev_=ev_):
                    b.wait(v, "t", t_trf)
                    b.wait(v, "gq", gw)
                    b.inc(v.tensor_copy(ev_, ps_trf[:]), "v", 1)
                block.vector(_)
                vv = b.cnt["v"]

                def _(gp, si=si, vv=vv, ev_=ev_):
                    b.wait(gp, "v", vv)
                    b.dma(gp, "gq", out=sfc_d[si].transpose([1, 0, 2]),
                                       in_=ev_
                                       .rearrange("p (k c) -> p k c", k=ND))
                block.gpsimd(_)
                g_fc[si % 2] = b.cnt["gq"]

            g_pj = [0, 0]
            for j in range(ND):
                vprev = b.cnt["v"]
                gprev = b.cnt["gq"]

                def _(sy, j=j, vprev=vprev, gprev=gprev):
                    b.wait(sy, "v", vprev)
                    b.wait(sy, "gq", gprev)
                    for qq in range(4):
                        b.dma(sy, "ld",
                              out=ttf4[:, qq * 1024:(qq + 1) * 1024],
                              in_=wpj_d[j * 128:(j + 1) * 128,
                                        qq * 1024:(qq + 1) * 1024])
                block.sync(_)
                ldv = b.cnt["ld"]

                def _(sc, j=j, ldv=ldv):
                    b.wait(sc, "ld", ldv)
                    b.inc(sc.activation(abs4, ttf4, AF.Abs,
                                        accum_out=sc_pj[:, j:j + 1]), "s", 1)
                block.scalar(_)
                sv = b.cnt["s"]
                tprev = b.cnt["t"]

                def _(v, j=j, sv=sv, tprev=tprev):
                    b.wait(v, "s", sv)
                    b.wait(v, "t", tprev)
                    b.wait(v, "v")
                    b.inc(v.tensor_scalar(sc_pj[:, j:j + 1], sc_pj[:, j:j + 1],
                                          INV_FF, None, ALU.mult), "v", 1)
                    b.wait(v, "v")
                    b.inc(v.tensor_scalar(sc_pj[:, j:j + 1], sc_pj[:, j:j + 1],
                                          1e-5, None, ALU.max), "v", 1)
                    b.wait(v, "v")
                    v.tensor_scalar(thr_s[:, 0:1], sc_pj[:, j:j + 1], 0.5, None, ALU.mult)
                    b.inc(v.tensor_scalar(thr_s[:, 1:2], sc_pj[:, j:j + 1], -0.5,
                                          None, ALU.mult), "v", 1)
                    for q in range(4):
                        cols = slice(q * 1024, (q + 1) * 1024)
                        b.wait(v, "v")
                        b.inc(v.tensor_scalar(ab_s, ttf4[:, cols], thr_s[:, 0:1],
                                              None, ALU.is_gt), "v", 1)
                        b.wait(v, "v")
                        b.inc(v.scalar_tensor_tensor(
                            tern_b[:, q * 1024:(q + 1) * 1024],
                            ttf4[:, cols], thr_s[:, 1:2], ab_s,
                            ALU.is_lt, ALU.subtract), "v", 1)
                block.vector(_)
                vv = b.cnt["v"]

                for q in range(4):
                    def _(t, q=q, vv=vv):
                        b.wait(t, "v", b.cnt["v"])
                        last = None
                        for kk in range(ND):
                            i_blk = q * ND + kk
                            last = t.transpose(ps_trf[:, kk * 128:(kk + 1) * 128],
                                               tern_b[:, i_blk * 128:(i_blk + 1) * 128],
                                               ident_b[:])
                        b.inc(last, "t", 1)
                    block.tensor(_)
                    tv = b.cnt["t"]
                    ev_ = evac_pj[q % 2]

                    def _(v, tv=tv, gw=g_pj[q % 2], ev_=ev_):
                        b.wait(v, "t", tv)
                        b.wait(v, "gq", gw)
                        b.inc(v.tensor_copy(ev_, ps_trf[:]), "v", 1)
                    block.vector(_)
                    vv2 = b.cnt["v"]

                    def _(gp, j=j, q=q, vv2=vv2, ev_=ev_):
                        b.wait(gp, "v", vv2)
                        dst = spj_d[j // 4, q * ND:(q + 1) * ND, j % 4]
                        b.dma(gp, "gq", out=dst.transpose([1, 0, 2]),
                                           in_=ev_
                                           .rearrange("p (i c) -> p i c", i=ND))
                    block.gpsimd(_)
                    g_pj[q % 2] = b.cnt["gq"]

            gprev = b.cnt["gq"]

            def _(sy, gprev=gprev):
                b.wait(sy, "gq", gprev)
                b.dma(sy, "ld", out=ug_all[:].rearrange("p (s r) -> p s r", r=16),
                                   in_=u_d.rearrange("(s p) r -> s p r", p=128).transpose([1, 0, 2]))
            block.sync(_)
            ld_u = b.cnt["ld"]

            def _(v, ld_u=ld_u):
                b.wait(v, "ld", ld_u)
                b.wait(v, "v")
                b.inc(v.reciprocal(gs_fc[:], sc_fc[:]), "v", 1)
                b.wait(v, "v")
                b.inc(v.tensor_scalar(gs_fc[:], gs_fc[:], gate_b[:, 0:1], None,
                                      ALU.mult), "v", 1)
                b.wait(v, "v")
                b.inc(v.tensor_scalar(gs_fc[:], gs_fc[:], -1.0, None, ALU.mult),
                      "v", 1)
                b.wait(v, "v")
                v.tensor_scalar(scn_fc[:], sc_fc[:], -1.0, None, ALU.mult)
                b.inc(v.tensor_scalar(scn_pj[:], sc_pj[:], -1.0, None, ALU.mult),
                      "v", 1)
            block.vector(_)
        v_prep = b.cnt["v"]
        ph7b_cm.__exit__(None, None, None)
        ph7_cm.__exit__(None, None, None)
        if phases < 8:
            if tap == "prep":
                emit_tap(sc_fc[:])
            return nc

        with nc.sbuf_tensor("ph8", [128, 6672], F32) as P8, \
             nc.sbuf_tensor("ph8z", [128, 8192], BF16) as P8Z, \
             nc.sbuf_tensor("ph8u", [16, FF], F32) as uTg, \
             nc.sbuf_tensor("ph8w", [128, 2048], BF16) as WF, \
             nc.sbuf_tensor("ph8vp", [128, 256], BF16) as VP:
            yTg = P8[:, 0:4096]
            t1T = P8[:, 4096:4608]
            t2T = P8[:, 4608:5120]
            bcz = P8[:, 5120:6144]
            ytm = P8[:, 5120:6144]
            z1tmp = P8[:, 6144:6656]
            usm = P8[:, 6656:6672]
            fcb = [WF[:, 0:1024], WF[:, 1024:2048]]
            pjb2 = [WF[:, 0:1024], WF[:, 1024:2048]]
            vp_h, vp_l = VP[:, 0:128], VP[:, 128:256]

            def z1h(k):
                return P8Z[:, k * 1024:k * 1024 + 512]

            def z1l(k):
                return P8Z[:, k * 1024 + 512:(k + 1) * 1024]

            sE, vE = b.cnt["s"], b.cnt["v"]

            def _(v):
                b.wait(v, "v", v_prep)
                b.wait(v, "v")
                b.inc(v.tensor_copy(vp_h, vT_s[:]), "v", 1)
                b.wait(v, "v")
                b.inc(v.tensor_sub(vp_l, vT_s[:], vp_h), "v", 1)
            block.vector(_)
            v_vp = b.cnt["v"]

            with nc.psum_tensor("ps_u", [128, 128], F32) as ps_u:
                v_prev = v_prep
                for st in range(NF):
                    def _(v, st=st, v_prev=v_prev):
                        b.wait(v, "v", v_prev)
                        b.inc(v.tensor_scalar(usm,
                                              ug_all[:, st * 16:(st + 1) * 16],
                                              gs_fc[:, st:st + 1], None, ALU.mult),
                              "v", 1)
                    block.vector(_)
                    vv = b.cnt["v"]

                    def _(t, st=st, vv=vv):
                        b.wait(t, "v", vv)
                        b.inc(t.transpose(ps_u[0:16, 0:128], usm, ident[:]),
                              "t", 1)
                    block.tensor(_)
                    tv = b.cnt["t"]

                    def _(v, st=st, tv=tv):
                        b.wait(v, "t", tv)
                        b.inc(v.tensor_copy(uTg[:, st * 128:(st + 1) * 128],
                                            ps_u[0:16, 0:128]), "v", 1)
                    block.vector(_)
                    v_prev = b.cnt["v"]
            v_u = b.cnt["v"]

            v_group_prev = v_u
            g_out = 0
            for g in range(G):
                with nc.psum_tensor(f"ps_bz_{g}", [128, 1024], F32) as ps_bz:
                    def _(t, g=g, v_group_prev=v_group_prev):
                        b.wait(t, "v", v_group_prev)
                        cols = slice(g * GS, (g + 1) * GS)
                        t.matmul(ps_bz[:, 0:512], ones_r[0:1, :], rows[0:1, cols],
                                 start=True, stop=True)
                        b.inc(t.matmul(ps_bz[:, 512:1024], ones_r[32:33, :],
                                       rows[32:33, cols],
                                       start=True, stop=True), "t", 1)
                    block.tensor(_)
                    tv = b.cnt["t"]
                    gw = g_out

                    def _(v, tv=tv, gw=gw):
                        b.wait(v, "t", tv)
                        b.wait(v, "gq", gw)
                        b.inc(v.tensor_copy(bcz, ps_bz[:]), "v", 1)
                    block.vector(_)
                    v_bcz = b.cnt["v"]

                for k in range(ND):
                    t_zf = b.cnt["t"]

                    def _(v, k=k, g=g, v_bcz=v_bcz, t_zf=t_zf):
                        b.wait(v, "v", v_bcz)
                        b.wait(v, "v")
                        if k == 0:
                            b.wait(v, "t", t_zf)
                        b.inc(v.tensor_sub(z1tmp, A_c(k)[:, g * GS:(g + 1) * GS],
                                           bcz[:, 0:512]), "v", 1)
                        b.wait(v, "v")
                        b.inc(v.tensor_mul(z1tmp, z1tmp, bcz[:, 512:1024]), "v", 1)
                    block.vector(_)
                    vv = b.cnt["v"]

                    def _(v, k=k, vv=vv):
                        b.wait(v, "v", vv)
                        b.inc(v.tensor_copy(z1h(k), z1tmp), "v", 1)
                        b.wait(v, "v")
                        b.inc(v.tensor_sub(z1l(k), z1tmp, z1h(k)), "v", 1)
                    block.vector(_)
                v_z1 = b.cnt["v"]

                with nc.psum_tensor(f"ps_lr_{g}", [16, 512], F32) as ps_lr:
                    def _(t, v_z1=v_z1):
                        b.wait(t, "v", max(v_z1, v_vp))
                        last = None
                        for pi, (vv_, zz) in enumerate(
                                ((vp_h, z1h), (vp_h, z1l), (vp_l, z1h))):
                            for k in range(ND):
                                last = t.matmul(ps_lr[:],
                                                vv_[:, k * 16:(k + 1) * 16],
                                                zz(k),
                                                start=(pi == 0 and k == 0),
                                                stop=(pi == 2 and k == ND - 1))
                        b.inc(last, "t", 1)
                    block.tensor(_)
                    tv = b.cnt["t"]

                    def _(v, tv=tv):
                        b.wait(v, "t", tv)
                        b.inc(v.tensor_copy(t1T[0:16, :], ps_lr[:]), "v", 1)
                    block.vector(_)
                    vv = b.cnt["v"]

                    def _(t, vv=vv):
                        b.wait(t, "v", vv)
                        b.inc(t.matmul(ps_lr[:], fbT[:], t1T[0:16, :],
                                       start=True, stop=True), "t", 1)
                    block.tensor(_)
                    tv = b.cnt["t"]

                    def _(v, tv=tv):
                        b.wait(v, "t", tv)
                        b.inc(v.tensor_copy(t2T[0:16, :], ps_lr[:]), "v", 1)
                    block.vector(_)
                    v_t2 = b.cnt["v"]

                with nc.psum_tensor(f"ps_h2_{g}", [128, 1536], F32) as ps_h2:
                    t_ent = b.cnt["t"]
                    s_rr = [0, 0]
                    v_rr_last = 0
                    t_fc = [t_ent, t_ent]
                    for i in range(NF):
                        ws_ = t_fc[i % 2]

                        def _(sy, i=i, ws_=ws_):
                            b.wait(sy, "t", ws_)
                            b.dma(sy, "ld", 
                                out=fcb[i % 2].rearrange("p (k c) -> p k c", k=ND),
                                in_=sfc_d[i].transpose([1, 0, 2]))
                        block.sync(_)
                        ldv = b.cnt["ld"]
                        sg = s_rr[i % 2]

                        def _(t, i=i, ldv=ldv, sg=sg, v_t2=v_t2, v_z1=v_z1):
                            b.wait(t, "ld", ldv)
                            b.wait(t, "v", max(v_t2, v_z1))
                            b.wait(t, "s", sg)
                            dst = ps_h2[:, (i % 2) * 512:(i % 2) * 512 + 512]
                            for pi, zz in enumerate((z1h, z1l)):
                                for k in range(ND):
                                    t.matmul(dst, fcb[i % 2][:, k * 128:(k + 1) * 128],
                                             zz(k), start=(pi == 0 and k == 0),
                                             stop=False)
                            b.inc(t.matmul(dst, uTg[:, i * 128:(i + 1) * 128],
                                           t2T[0:16, :], start=False, stop=True),
                                  "t", 1)
                        block.tensor(_)
                        t_fc[i % 2] = b.cnt["t"]
                        tv = b.cnt["t"]

                        def _(sc, i=i, tv=tv, v_group_prev=v_group_prev,
                              vr=v_rr_last):
                            b.wait(sc, "t", tv)
                            b.wait(sc, "v", v_group_prev)
                            b.wait(sc, "v", vr)
                            b.wait(sc, "s")
                            src = ps_h2[:, (i % 2) * 512:(i % 2) * 512 + 512]
                            b.inc(sc.activation(bcz[:, 0:512], src, AF.Relu,
                                                scale=scn_fc[:, i:i + 1]), "s", 1)
                            b.wait(sc, "s")
                            b.inc(sc.activation(ps_h2[:, 1024:1536], bcz[:, 0:512],
                                                AF.Square), "s", 1)
                        block.scalar(_)
                        sv2 = b.cnt["s"]

                        def _(v, i=i, sv2=sv2):
                            b.wait(v, "s", sv2)
                            b.wait(v, "v")
                            b.inc(v.tensor_copy(B16[:, i * 1024:i * 1024 + 512],
                                                ps_h2[:, 1024:1536]), "v", 1)
                            b.wait(v, "v")
                            b.inc(v.tensor_sub(B16[:, i * 1024 + 512:i * 1024 + 1024],
                                               ps_h2[:, 1024:1536],
                                               B16[:, i * 1024:i * 1024 + 512]),
                                  "v", 1)
                        block.vector(_)
                        v_rr_last = b.cnt["v"]
                        s_rr[i % 2] = b.cnt["s"]
                s_rrd = b.cnt["s"]
                v_rrd = b.cnt["v"]

                t_y_half = []
                for half in range(2):
                    with nc.psum_tensor(f"ps_y_{g}_{half}", [128, 2048], F32) as ps_y:
                        t_half0 = b.cnt["t"]
                        v_half0 = b.cnt["v"]
                        t_pj = [t_half0, t_half0]
                        ldv = 0
                        for i in range(NF):
                            if i % 2 == 0:
                                par = (i // 2) % 2

                                def _(sy, half=half, i=i, par=par, ws_=t_pj[par],
                                      s_rrd=s_rrd):
                                    b.wait(sy, "t", ws_)
                                    if half == 0 and i < 4:
                                        b.wait(sy, "s", s_rrd)
                                    src2 = spj_d[half, i:i + 2] \
                                        .rearrange("i2 jj p c -> (i2 jj) p c") \
                                        .transpose([1, 0, 2])
                                    b.dma(sy, "ld",
                                          out=pjb2[par].rearrange("p (m c) -> p m c", m=8),
                                          in_=src2)
                                block.sync(_)
                                ldv = b.cnt["ld"]

                            def _(t, i=i, ldv=ldv, s_rrd=s_rrd, v_half0=v_half0,
                                  v_rrd=v_rrd):
                                b.wait(t, "ld", ldv)
                                b.wait(t, "s", s_rrd)
                                b.wait(t, "v", v_rrd)
                                b.wait(t, "v", v_half0)
                                rrh = B16[:, i * 1024:i * 1024 + 512]
                                rrl = B16[:, i * 1024 + 512:i * 1024 + 1024]
                                buf = pjb2[(i // 2) % 2]
                                off = (i % 2) * 512
                                last = None
                                for jj in range(4):
                                    for pi, mm in enumerate((rrh, rrl)):
                                        last = t.matmul(
                                            ps_y[:, jj * 512:(jj + 1) * 512],
                                            buf[:, off + jj * 128:off + (jj + 1) * 128],
                                            mm, start=(i == 0 and pi == 0),
                                            stop=(i == NF - 1 and pi == 1))
                                b.inc(last, "t", 1)
                            block.tensor(_)
                            t_pj[(i // 2) % 2] = b.cnt["t"]
                        t_y = b.cnt["t"]

                        for jj in range(4):
                            j = half * 4 + jj

                            def _(v, g=g, j=j, jj=jj, t_y=t_y, v_z1=v_z1):
                                b.wait(v, "t", t_y)
                                b.wait(v, "v", v_z1)
                                b.inc(v.scalar_tensor_tensor(
                                    yTg[:, j * 512:(j + 1) * 512],
                                    ps_y[:, jj * 512:(jj + 1) * 512],
                                    scn_pj[:, j:j + 1],
                                    A_c(j)[:, g * GS:(g + 1) * GS],
                                    ALU.mult, ALU.add), "v", 1)
                            block.vector(_)
                v_y = b.cnt["v"]

                with nc.psum_tensor(f"ps_ty_{g}", [128, 1024], F32) as ps_ty:
                    v_prev_ty = 0
                    for tt in range(NTG):
                        def _(t, tt=tt, v_y=v_y, v_prev_ty=v_prev_ty):
                            b.wait(t, "v", v_y)
                            if v_prev_ty:
                                b.wait(t, "v", v_prev_ty)
                            last = None
                            for c in range(ND):
                                last = t.transpose(
                                    ps_ty[:, c * 128:(c + 1) * 128],
                                    yTg[:, c * 512 + tt * 128: c * 512 + (tt + 1) * 128],
                                    ident[:])
                            b.inc(last, "t", 1)
                        block.tensor(_)
                        tv = b.cnt["t"]
                        gw = g_out

                        def _(v, tv=tv, gw=gw):
                            b.wait(v, "t", tv)
                            b.wait(v, "gq", gw)
                            b.inc(v.tensor_copy(ytm, ps_ty[:]), "v", 1)
                        block.vector(_)
                        v_prev_ty = b.cnt["v"]

                        def _(gp, g=g, tt=tt, v_prev_ty=v_prev_ty):
                            b.wait(gp, "v", v_prev_ty)
                            r0 = g * GS + tt * 128
                            b.dma(gp, "gq", out=y_d[r0:r0 + 128, :], in_=ytm)
                        block.gpsimd(_)
                        g_out = b.cnt["gq"]
                v_group_prev = b.cnt["v"]

            def _(gp):
                b.wait(gp, "gq")
            block.gpsimd(_)
        return nc


_W_NAMES = [("wq", "L0_attn_Wq"), ("wk", "L0_attn_Wk"), ("wv", "L0_attn_Wv"),
            ("wo", "L0_attn_Wo"), ("w2", "L0_attn_W2"), ("wfc", "L0_ffn_w_fc"),
            ("wpj", "L0_ffn_w_proj"), ("u", "L0_ffn_u"), ("vw", "L0_ffn_v"),
            ("fb", "L0_ffn_fast_b"),
            ("bq", "L0_attn_bq"), ("bk", "L0_attn_bk"), ("bv", "L0_attn_bv"),
            ("bo", "L0_attn_bo"), ("b1", "L0_attn_b1"), ("b2", "L0_attn_b2"),
            ("g1", "L0_attn_g1"), ("be1", "L0_attn_be1"), ("g2", "L0_attn_g2"),
            ("be2", "L0_attn_be2")]


def make_inputs(kw):
    common = {}
    for short, full in _W_NAMES:
        common[short] = np.ascontiguousarray(np.asarray(kw[full], np.float32))
    w1 = np.asarray(kw["L0_attn_W1"], np.float32)
    common["w1t"] = np.ascontiguousarray(
        w1.reshape(ND, 128, NF, 128).transpose(2, 0, 1, 3))
    common["gate"] = np.ascontiguousarray(
        np.asarray(kw["L0_ffn_gate"], np.float32).reshape(1, 1))
    common["ident_in"] = np.eye(128, dtype=np.float32)
    x = np.asarray(kw["L0_attn_x"], np.float32)
    return common, x


def kernel(**kw):
    common, x = make_inputs(kw)
    nc = build_program(tap=DEBUG_TAP, phases=BUILD_PHASES)
    in_maps = [dict(common, x=np.ascontiguousarray(x[bb])) for bb in range(8)]
    res = run_bass_kernel_spmd(nc, in_maps, core_ids=list(range(8)))
    return np.stack([r["y"] for r in res.results], axis=0)


# version 17 (final)
# speedup vs baseline: 1.0509x; 1.0509x over previous
import contextlib
import numpy as np

import concourse.bass as bass
import concourse.mybir as mybir
from concourse.bass_utils import run_bass_kernel_spmd

F32 = mybir.dt.float32
BF16 = mybir.dt.bfloat16
AF = mybir.ActivationFunctionType
ALU = mybir.AluOpType
AX = mybir.AxisListType

S, D, FF, NH, HD = 2048, 1024, 4096, 16, 64
ND, NF, NT = D // 128, FF // 128, S // 128
G, GS = 4, 512
NTG = GS // 128
INV_D = 1.0 / D
INV_FF = 1.0 / FF
EPS_OUT = 1e-5
EPS_IN = 1e-12
_S32 = np.float64(np.float32(np.sqrt(np.float32(2.0))))
_S64 = np.sqrt(np.float64(2.0))
KAPPA = float(np.float32((_S64 - _S32) / _S32))
KAPPA2 = float(np.float32((_S64 - _S32) / _S64))

DEBUG_TAP = None
BUILD_PHASES = 99


RR = {"ld": 4, "gq": 2}


class Bld:
    def __init__(self, nc, block):
        self.nc, self.block = nc, block
        self.sem, self.cnt = {}, {}
        self.snap = {}
        self.rri = {}

    def add_sem(self, name, handle):
        self.sem[name] = handle
        self.cnt[name] = 0

    def add_rr(self, base, handles):
        for j, h in enumerate(handles):
            self.add_sem(f"{base}{j}", h)
        self.cnt[base] = 0
        self.snap[base] = {0: tuple(0 for _ in handles)}
        self.rri[base] = 0

    def dma(self, eng, base, out, in_, **kw):
        j = self.rri[base] % RR[base]
        self.rri[base] += 1
        s = f"{base}{j}"
        self.wait(eng, s)
        instr = eng.dma_start(out=out, in_=in_, **kw)
        instr.then_inc(self.sem[s], 16)
        self.cnt[s] += 16
        self.cnt[base] += 16
        self.snap[base][self.cnt[base]] = tuple(
            self.cnt[f"{base}{k}"] for k in range(RR[base]))
        return self.cnt[base]

    def inc(self, instr, name, n=1):
        assert name not in self.snap
        instr.then_inc(self.sem[name], n)
        self.cnt[name] += n
        return self.cnt[name]

    def wait(self, eng, name, val=None):
        if val is None:
            val = self.cnt[name]
        assert val <= self.cnt[name], \
            f"wait {name}>={val} but only {self.cnt[name]} emitted"
        if val == 0:
            return
        if name in self.snap:
            for k, c in enumerate(self.snap[name][val]):
                if c > 0:
                    eng.wait_ge(self.sem[f"{name}{k}"], c)
            return
        eng.wait_ge(self.sem[name], val)


def build_program(tap=None, phases=99):
    nc = bass.Bass("TRN2", target_bir_lowering=False)

    x_d = nc.dram_tensor("x", [S, D], F32, kind="ExternalInput").ap()
    wq_d = nc.dram_tensor("wq", [D, D], F32, kind="ExternalInput").ap()
    wk_d = nc.dram_tensor("wk", [D, D], F32, kind="ExternalInput").ap()
    wv_d = nc.dram_tensor("wv", [D, D], F32, kind="ExternalInput").ap()
    wo_d = nc.dram_tensor("wo", [D, D], F32, kind="ExternalInput").ap()
    w1t_d = nc.dram_tensor("w1t", [NF, ND, 128, 128], F32, kind="ExternalInput").ap()
    w2_d = nc.dram_tensor("w2", [FF, D], F32, kind="ExternalInput").ap()
    wfc_d = nc.dram_tensor("wfc", [FF, D], F32, kind="ExternalInput").ap()
    wpj_d = nc.dram_tensor("wpj", [D, FF], F32, kind="ExternalInput").ap()
    u_d = nc.dram_tensor("u", [FF, 16], F32, kind="ExternalInput").ap()
    vw_d = nc.dram_tensor("vw", [16, D], F32, kind="ExternalInput").ap()
    fb_d = nc.dram_tensor("fb", [16, 16], F32, kind="ExternalInput").ap()
    gate_d = nc.dram_tensor("gate", [1, 1], F32, kind="ExternalInput").ap()
    ident_d = nc.dram_tensor("ident_in", [128, 128], F32, kind="ExternalInput").ap()
    bias_d = {}
    for nm, n in [("bq", D), ("bk", D), ("bv", D), ("bo", D), ("b1", FF),
                  ("b2", D), ("g1", D), ("be1", D), ("g2", D), ("be2", D)]:
        bias_d[nm] = nc.dram_tensor(nm, [n], F32, kind="ExternalInput").ap()
    y_d = nc.dram_tensor("y", [S, D], F32, kind="ExternalOutput").ap()
    sfc_d = nc.dram_tensor("spill_fc", [NF, ND, 128, 128], BF16).ap()
    w1p_d = nc.dram_tensor("spill_w1p", [2, NF, ND, 128, 128], BF16).ap()
    w2p_d = nc.dram_tensor("spill_w2p", [2, NF, 128, D], BF16).ap()
    spj_d = nc.dram_tensor("spill_pj", [2, NF, 4, 128, 128], BF16).ap()
    tap_d = None
    if tap is not None:
        tap_d = nc.dram_tensor("tap", [128, ND * S], F32, kind="ExternalOutput").ap()

    with contextlib.ExitStack() as ctx:
        en = ctx.enter_context

        A_t = en(nc.sbuf_tensor("strA", [128, ND * S], F32))
        B16 = en(nc.sbuf_tensor("strB16", [128, 2 * ND * S], BF16))
        ident = en(nc.sbuf_tensor("ident", [128, 128], F32))
        ident_b = en(nc.sbuf_tensor("ident_b", [128, 128], BF16))
        ones_c = en(nc.sbuf_tensor("ones_c", [128, 1], F32))
        ones_r = en(nc.sbuf_tensor("ones_r", [33, 128], F32))
        pcol = en(nc.sbuf_tensor("pcol", [128, 6 * ND + NF], F32))
        PC = {}
        off = 0
        for nm, n in [("bo", ND), ("b2", ND), ("g1", ND), ("be1", ND),
                      ("g2", ND), ("be2", ND), ("b1", NF)]:
            PC[nm] = (off, n)
            off += n
        gate_b = en(nc.sbuf_tensor("gate_b", [128, 1], F32))
        kv_s = en(nc.sbuf_tensor("kv_s", [128, 512], F32))
        kvp16 = en(nc.sbuf_tensor("kvp16", [128, 1024], BF16))
        sc_fc = en(nc.sbuf_tensor("sc_fc", [128, NF], F32))
        scn_fc = en(nc.sbuf_tensor("scn_fc", [128, NF], F32))
        scn_pj = en(nc.sbuf_tensor("scn_pj", [128, ND], F32))
        gs_fc = en(nc.sbuf_tensor("gs_fc", [128, NF], F32))
        sc_pj = en(nc.sbuf_tensor("sc_pj", [128, ND], F32))
        vT_s = en(nc.sbuf_tensor("vT_s", [128, ND * 16], F32))
        fbT = en(nc.sbuf_tensor("fbT", [16, 16], F32))
        rows = en(nc.sbuf_tensor("rows", [64, S], F32))
        epsc = en(nc.sbuf_tensor("epsc", [128, 2], F32))

        ld_sems = [en(nc.semaphore(f"ld{j}")) for j in range(RR["ld"])]
        gq_sems = [en(nc.semaphore(f"gq{j}")) for j in range(RR["gq"])]
        ts_ = en(nc.semaphore("ts"))
        vs = en(nc.semaphore("vs"))
        ss = en(nc.semaphore("ss"))
        block = en(nc.Block())
        b = Bld(nc, block)
        for nm, h in [("t", ts_), ("v", vs), ("s", ss)]:
            b.add_sem(nm, h)
        b.add_rr("ld", ld_sems)
        b.add_rr("gq", gq_sems)

        def A_c(c):
            return A_t[:, c * S:(c + 1) * S]


        kvp_h = kvp16[:, 0:512]
        kvp_l = kvp16[:, 512:1024]

        def B_slot16(i):
            return B16[:, i * 1024:(i + 1) * 1024]

        def Bp_hc(c):
            return B16[:, c * 2 * S: c * 2 * S + S]

        def Bp_lc(c):
            return B16[:, c * 2 * S + S: (c + 1) * 2 * S]

        def pc(nm, c):
            o, n = PC[nm]
            return pcol[:, o + c:o + c + 1]

        def emit_tap(src_ap):
            cv, cs, ct = b.cnt["v"], b.cnt["s"], b.cnt["t"]

            def _(gp):
                b.wait(gp, "v", cv)
                b.wait(gp, "s", cs)
                b.wait(gp, "t", ct)
                b.dma(gp, "gq", out=tap_d[0:src_ap.shape[0], 0:src_ap.shape[-1]],
                                   in_=src_ap)
                b.wait(gp, "gq")
            block.gpsimd(_)

        setup_marks = {}

        def _(sy):
            off = 0
            for nm in ["bo", "b2", "g1", "be1", "g2", "be2", "b1"]:
                nch = PC[nm][1]
                b.dma(sy, "ld", out=A_t[0:nch, off:off + 128],
                                   in_=bias_d[nm].rearrange("(c p) -> c p", p=128))
                off += 128
            b.dma(sy, "ld", out=gate_b[:], in_=gate_d.partition_broadcast(128))
            b.dma(sy, "ld", out=fbT[:], in_=fb_d)
            b.dma(sy, "ld", out=A_t[0:16, 1024:2048], in_=vw_d)
            b.dma(sy, "ld", out=ident[:], in_=ident_d)
        block.sync(_)
        ld_par = b.cnt["ld"]

        def _(v):
            v.memset(ones_c[:], 1.0)
            v.memset(epsc[:, 0:1], EPS_OUT)
            v.memset(epsc[:, 1:2], EPS_IN)
            b.inc(v.memset(ones_r[:], 1.0), "v", 1)
        block.vector(_)

        def _(sc):
            b.wait(sc, "ld", ld_par)
            b.inc(sc.copy(ident_b[:], ident[:]), "s", 1)
        block.scalar(_)

        with nc.psum_tensor("ps_m", [128, 128], F32) as ps_m:
            off = 0
            for nm in ["bo", "b2", "g1", "be1", "g2", "be2", "b1"]:
                nch = PC[nm][1]
                vprev = b.cnt["v"]

                def _(t, nch=nch, off=off, vprev=vprev):
                    b.wait(t, "ld", ld_par)
                    b.wait(t, "v", vprev)
                    b.inc(t.transpose(ps_m[:, 0:nch], A_t[0:nch, off:off + 128],
                                      ident[0:nch, 0:nch]), "t", 1)
                block.tensor(_)
                tv = b.cnt["t"]

                def _(v, nm=nm, nch=nch, tv=tv):
                    b.wait(v, "t", tv)
                    o = PC[nm][0]
                    b.inc(v.tensor_copy(pcol[:, o:o + nch], ps_m[:, 0:nch]), "v", 1)
                    if nm == "b1":
                        b.wait(v, "v")
                        b.inc(v.scalar_tensor_tensor(
                            pcol[:, o:o + nch], ps_m[:, 0:nch], KAPPA,
                            pcol[:, o:o + nch], ALU.mult, ALU.add), "v", 1)
                block.vector(_)
                off += 128
            vprev = b.cnt["v"]

            def _(t, vprev=vprev):
                b.wait(t, "v", vprev)
                b.inc(t.transpose(ps_m[0:16, 0:16], fbT[:], ident[0:16, 0:16]), "t", 1)
            block.tensor(_)
            tv = b.cnt["t"]

            def _(v, tv=tv):
                b.wait(v, "t", tv)
                b.inc(v.tensor_copy(fbT[:], ps_m[0:16, 0:16]), "v", 1)
            block.vector(_)
            for k in range(ND):
                vprev = b.cnt["v"]

                def _(t, k=k, vprev=vprev):
                    b.wait(t, "v", vprev)
                    b.inc(t.transpose(ps_m[:, 0:16],
                                      A_t[0:16, 1024 + k * 128:1024 + (k + 1) * 128],
                                      ident[0:16, 0:16]), "t", 1)
                block.tensor(_)
                tv = b.cnt["t"]

                def _(v, k=k, tv=tv):
                    b.wait(v, "t", tv)
                    b.inc(v.tensor_copy(vT_s[:, k * 16:(k + 1) * 16], ps_m[:, 0:16]),
                          "v", 1)
                block.vector(_)
        setup_marks["v"] = b.cnt["v"]
        setup_marks["t"] = b.cnt["t"]
        if phases < 0:
            if tap == "setup":
                emit_tap(pcol[:])
            return nc

        with nc.sbuf_tensor("ph0", [128, 5376], F32) as P0, \
             nc.psum_tensor("ps_tr0", [128, 1024], F32) as ps_tr:
            xt = [P0[:, 0:1024], P0[:, 1024:2048]]
            zt = [P0[:, 2048:3072], P0[:, 3072:4096]]
            sqs = P0[:, 4096:5120]
            st = P0[:, 5120:5376]
            xsum, xsq = st[:, 0:16], st[:, 16:32]
            mu16, rstd16 = st[:, 32:48], st[:, 48:64]
            nmr16, var16 = st[:, 64:80], st[:, 80:96]
            done = [(0, 0)] * NT
            for si in range(NT):
                xb = xt[si % 2]
                wv_, ws_ = done[si - 2] if si >= 2 else (0, 0)

                def _(sy, si=si, xb=xb, wv_=wv_, ws_=ws_):
                    b.wait(sy, "v", wv_)
                    b.wait(sy, "s", ws_)
                    b.dma(sy, "ld", out=xb, in_=x_d[si * 128:(si + 1) * 128, :])
                block.sync(_)
                ldv = b.cnt["ld"]

                def _(v, si=si, xb=xb, ldv=ldv):
                    b.wait(v, "ld", ldv)
                    b.inc(v.reduce_sum(xsum[:, si:si + 1], xb, axis=AX.X), "v", 1)
                block.vector(_)

                def _(sc, si=si, xb=xb, ldv=ldv):
                    b.wait(sc, "ld", ldv)
                    b.wait(sc, "s")
                    b.inc(sc.activation(sqs, xb, AF.Square,
                                        accum_out=xsq[:, si:si + 1]), "s", 1)
                block.scalar(_)
                done[si] = (b.cnt["v"], b.cnt["s"])

            sstat = b.cnt["s"]

            def _(v, sstat=sstat):
                b.wait(v, "s", sstat)
                b.wait(v, "v")
                b.inc(v.tensor_scalar(mu16, xsum, INV_D, None, ALU.mult), "v", 1)
                b.wait(v, "v")
                b.inc(v.tensor_mul(var16, mu16, mu16), "v", 1)
                b.wait(v, "v")
                b.inc(v.scalar_tensor_tensor(var16, xsq, INV_D, var16,
                                             ALU.mult, ALU.subtract), "v", 1)
            block.vector(_)
            vv = b.cnt["v"]

            def _(sc, vv=vv):
                b.wait(sc, "v", vv)
                b.wait(sc, "s")
                b.inc(sc.activation(rstd16, var16, AF.Ln, bias=epsc[:, 0:1]), "s", 1)
                b.wait(sc, "s")
                b.inc(sc.activation(rstd16, rstd16, AF.Exp, scale=-0.5), "s", 1)
            block.scalar(_)
            sv = b.cnt["s"]

            def _(v, sv=sv):
                b.wait(v, "s", sv)
                b.inc(v.tensor_mul(var16, mu16, rstd16), "v", 1)
                b.wait(v, "v")
                b.inc(v.tensor_scalar(nmr16, var16, -1.0, None, ALU.mult), "v", 1)
            block.vector(_)
            v_st0 = b.cnt["v"]

            t_tr = [0, 0]
            v_ev = 0
            for si in range(NT):
                xb, zb = xt[si % 2], zt[si % 2]
                wv_ = v_ev if si >= 2 else 0

                def _(sy, si=si, xb=xb, wv_=wv_):
                    b.wait(sy, "v", max(wv_, v_st0))
                    b.dma(sy, "ld", out=xb, in_=x_d[si * 128:(si + 1) * 128, :])
                block.sync(_)
                ldv = b.cnt["ld"]
                wtr = t_tr[si % 2]

                def _(v, si=si, xb=xb, zb=zb, ldv=ldv, wtr=wtr):
                    b.wait(v, "ld", ldv)
                    b.wait(v, "t", wtr)
                    b.inc(v.tensor_scalar(zb, xb, rstd16[:, si:si + 1],
                                          nmr16[:, si:si + 1], ALU.mult, ALU.add),
                          "v", 1)
                block.vector(_)
                vz = b.cnt["v"]

                def _(t, si=si, zb=zb, vz=vz, v_ev=v_ev):
                    b.wait(t, "v", vz)
                    if si > 0:
                        b.wait(t, "v", v_ev)
                    last = None
                    for c in range(ND):
                        last = t.transpose(ps_tr[:, c * 128:(c + 1) * 128],
                                           zb[:, c * 128:(c + 1) * 128], ident[:])
                    b.inc(last, "t", 1)
                block.tensor(_)
                t_tr[si % 2] = b.cnt["t"]
                tv = b.cnt["t"]

                def _(v, si=si, tv=tv):
                    b.wait(v, "t", tv)
                    dst = A_t[:].rearrange("p (c t) -> p c t", c=ND)[:, :, si * 128:(si + 1) * 128]
                    b.inc(v.tensor_copy(dst, ps_tr[:].rearrange("p (c f) -> p c f", c=ND)),
                          "v", 1)
                block.vector(_)
                v_ev = b.cnt["v"]
        v_z0 = b.cnt["v"]
        if phases < 1:
            if tap == "z0T":
                emit_tap(A_t[:])
            return nc

        _ph1s_box = {}

        def _ph1s(name):
            return _ph1s_box[name]

        def elu_norm(blk):
            e1, e2, rst = _ph1s("e1"), _ph1s("e2"), _ph1s("rst")
            s_entry = b.cnt["s"]

            def _(v, s_entry=s_entry):
                b.wait(v, "s", s_entry)
                b.wait(v, "v")
                b.inc(v.tensor_scalar(e1, blk, 0.0, None, ALU.min), "v", 1)
            block.vector(_)
            vv = b.cnt["v"]

            def _(sc, vv=vv):
                b.wait(sc, "v", vv)
                b.inc(sc.activation(e2, e1, AF.Exp), "s", 1)
            block.scalar(_)
            sv = b.cnt["s"]

            def _(v, sv=sv):
                b.wait(v, "s", sv)
                b.inc(v.tensor_scalar(e1, blk, 0.0, None, ALU.max), "v", 1)
                b.wait(v, "v")
                b.inc(v.scalar_tensor_tensor(blk, e2, -1.0, e1, ALU.add, ALU.add),
                      "v", 1)
            block.vector(_)
            vv = b.cnt["v"]

            def _(sc, vv=vv):
                b.wait(sc, "v", vv)
                b.inc(sc.activation(e2, blk, AF.Square), "s", 1)
            block.scalar(_)
            sv = b.cnt["s"]

            def _(v, sv=sv):
                b.wait(v, "s", sv)
                b.inc(v.reduce_sum(rst[:, 0:16],
                                   e2.rearrange("p (h d) -> p h d", d=HD), axis=AX.X),
                      "v", 1)
            block.vector(_)
            vv = b.cnt["v"]

            def _(sc, vv=vv):
                b.wait(sc, "v", vv)
                b.inc(sc.activation(rst[:, 16:32], rst[:, 0:16], AF.Ln), "s", 1)
                b.wait(sc, "s")
                b.inc(sc.activation(rst[:, 16:32], rst[:, 16:32], AF.Exp, scale=-0.5),
                      "s", 1)
            block.scalar(_)
            sv = b.cnt["s"]

            def _(v, sv=sv):
                b.wait(v, "s", sv)
                rb = rst[:, 16:32].unsqueeze(2).broadcast_to([128, 16, HD])
                b.inc(v.tensor_tensor(blk.rearrange("p (h d) -> p h d", d=HD),
                                      blk.rearrange("p (h d) -> p h d", d=HD),
                                      rb, ALU.mult), "v", 1)
            block.vector(_)
            return b.cnt["v"]

        def load_resident_w_pairs(dst_h, dst_l, dram, wtmp, wait_v=0):
            s_done = [b.cnt["s"]] * 2
            v_done = [b.cnt["v"]] * 2
            for k in range(ND):
                tb = wtmp[:, (k % 2) * 1024:(k % 2) * 1024 + 1024]
                wv_, ws_ = v_done[k % 2], s_done[k % 2]

                def _(sy, k=k, tb=tb, wv_=wv_, ws_=ws_, wait_v=wait_v):
                    b.wait(sy, "v", max(wv_, wait_v))
                    b.wait(sy, "s", ws_)
                    b.dma(sy, "ld", out=tb, in_=dram[k * 128:(k + 1) * 128, :])
                block.sync(_)
                ldv = b.cnt["ld"]
                hs = dst_h.rearrange("p (k w) -> p k w", k=ND)[:, k, :]
                ls = dst_l.rearrange("p (k w) -> p k w", k=ND)[:, k, :]

                def _(v, tb=tb, hs=hs, ls=ls, ldv=ldv):
                    b.wait(v, "ld", ldv)
                    b.wait(v, "v")
                    b.inc(v.tensor_copy(hs, tb), "v", 1)
                    b.wait(v, "v")
                    b.inc(v.tensor_sub(ls, tb, hs), "v", 1)
                block.vector(_)
                v_done[k % 2] = b.cnt["v"]
                s_done[k % 2] = b.cnt["s"]
            return b.cnt["v"]

        def derive_pair_blocks(zh, zl, si, v_ready, t_free):
            srcs = A_t[:].rearrange("p (c t) -> p c t", c=ND)[:, :, si * 128:(si + 1) * 128]
            zh3 = zh.rearrange("p (c t) -> p c t", c=ND)
            zl3 = zl.rearrange("p (c t) -> p c t", c=ND)

            def _(v, v_ready=v_ready, t_free=t_free):
                b.wait(v, "v", v_ready)
                b.wait(v, "v")
                b.wait(v, "t", t_free)
                b.inc(v.tensor_copy(zh3, srcs), "v", 1)
                b.wait(v, "v")
                b.inc(v.tensor_sub(zl3, srcs, zh3), "v", 1)
            block.vector(_)
            return b.cnt["s"], b.cnt["v"]

        def tok_proj3(t, ps, wh, wl, zh, zl):
            last = None
            for j in range(2):
                cols = slice(j * 512, (j + 1) * 512)
                for pi, (zz, ww) in enumerate(((zh, wh), (zl, wh), (zh, wl))):
                    for k in range(ND):
                        last = t.matmul(
                            ps[:, cols],
                            zz[:, k * 128:(k + 1) * 128],
                            ww.rearrange("p (k w) -> p k w", k=ND)[:, k, cols],
                            start=(pi == 0 and k == 0),
                            stop=(pi == 2 and k == ND - 1))
            return last

        with nc.sbuf_tensor("ph1zp16", [128, 4096], BF16) as ZP16:
            bkb_cm = nc.sbuf_tensor("ph1bias", [128, 1024], F32)
            BKB = bkb_cm.__enter__()
            wp_cm = nc.sbuf_tensor("phwp", [128, 2048], F32)
            WP_scope = wp_cm.__enter__()
            wpb_cm = nc.sbuf_tensor("phwpb", [128, 4096], BF16)
            WPB_scope = wpb_cm.__enter__()
            ph1s_cm = nc.sbuf_tensor("ph1s", [128, 7296], F32)
            PH1S = ph1s_cm.__enter__()
            kbp_ = [PH1S[:, 0:1024], PH1S[:, 2048:3072]]
            vbp_ = [PH1S[:, 1024:2048], PH1S[:, 3072:4096]]
            _ph1s_box.update(e1=PH1S[:, 4096:5120], e2=PH1S[:, 6272:7296],
                             rst=PH1S[:, 5120:5248])
            bvb = PH1S[:, 5248:6272]
            wv_h, wv_l = B16[:, 0:8192], B16[:, 8192:16384]
            wk_h, wk_l = B16[:, 16384:24576], B16[:, 24576:32768]
            zpp = [(ZP16[:, 0:1024], ZP16[:, 1024:2048]),
                   (ZP16[:, 2048:3072], ZP16[:, 3072:4096])]
            bkb = BKB[:]
            wtmp_kv = PH1S[:, 0:2048]

            def _(sy):
                b.wait(sy, "v", v_z0)
                b.dma(sy, "ld", out=bvb,
                                   in_=bias_d["bv"].unsqueeze(0).partition_broadcast(128))
                b.dma(sy, "ld", out=bkb,
                                   in_=bias_d["bk"].unsqueeze(0).partition_broadcast(128))
            block.sync(_)
            ld_b = b.cnt["ld"]
            def _(sy):
                b.wait(sy, "t", setup_marks["t"])
                b.wait(sy, "v", setup_marks["v"])
            block.sync(_)
            v_wv = load_resident_w_pairs(wv_h, wv_l, wv_d, wtmp_kv)
            v_wk = load_resident_w_pairs(wk_h, wk_l, wk_d, wtmp_kv)
            WP = WP_scope
            wp_tmp = [WP[:, 0:1024], WP[:, 1024:2048]]
            wp_bp = [WPB_scope[:, 0:2048], WPB_scope[:, 2048:4096]]
            v_wp = [b.cnt["v"], b.cnt["v"]]
            s_wp = [b.cnt["s"], b.cnt["s"]]
            t_wp0 = b.cnt["t"]
            g_wp = [0, 0]

            def emit_prep_stripe(idx):
                w2_mode = idx >= NF
                i = idx % NF
                par = idx % 2
                tb = wp_tmp[par]
                wpb = wp_bp[par]
                src_ap = (w2_d[i * 128:(i + 1) * 128, :] if w2_mode
                          else w1t_d[i].transpose([1, 0, 2]))

                def _(sy, tb=tb, src_ap=src_ap, wv_=v_wp[par], ws_=s_wp[par],
                      idx=idx):
                    b.wait(sy, "v", wv_)
                    b.wait(sy, "s", ws_)
                    if idx < 2:
                        b.wait(sy, "t", t_wp0)
                    ob = tb
                    if src_ap.ndim == 3:
                        ob = tb.rearrange("p (k c) -> p k c", k=src_ap.shape[1])
                    b.dma(sy, "ld", out=ob, in_=src_ap)
                block.sync(_)
                ldv = b.cnt["ld"]

                def _(v, tb=tb, ldv=ldv, w2_mode=w2_mode):
                    b.wait(v, "ld", ldv)
                    b.wait(v, "v")
                    kap = -KAPPA2 if w2_mode else KAPPA
                    b.inc(v.scalar_tensor_tensor(tb, tb, kap, tb,
                                                 ALU.mult, ALU.add), "v", 1)
                block.vector(_)
                vk = b.cnt["v"]

                def _(sc, tb=tb, wpb=wpb, vk=vk, gw=g_wp[par]):
                    b.wait(sc, "v", vk)
                    b.wait(sc, "gq", gw)
                    b.wait(sc, "s")
                    b.inc(sc.copy(wpb[:, 0:1024], tb), "s", 1)
                block.scalar(_)
                sv = b.cnt["s"]
                s_wp[par] = sv

                def _(v, tb=tb, wpb=wpb, sv=sv):
                    b.wait(v, "s", sv)
                    b.inc(v.tensor_sub(wpb[:, 1024:2048], tb, wpb[:, 0:1024]),
                          "v", 1)
                block.vector(_)
                vv = b.cnt["v"]
                v_wp[par] = vv

                def _(gp, i=i, w2_mode=w2_mode, wpb=wpb, vv=vv):
                    b.wait(gp, "v", vv)
                    for hl in range(2):
                        srcv = wpb[:, hl * 1024:(hl + 1) * 1024]
                        if w2_mode:
                            dst = w2p_d[hl, i]
                        else:
                            dst = w1p_d[hl, i].transpose([1, 0, 2])
                            srcv = srcv.rearrange("p (k c) -> p k c", k=ND)
                        b.dma(gp, "gq", out=dst, in_=srcv)
                block.gpsimd(_)
                g_wp[par] = b.cnt["gq"]


            with nc.psum_tensor("ps_v1", [128, 1024], F32) as ps_v, \
                 nc.psum_tensor("ps_k1", [128, 1024], F32) as ps_k, \
                 nc.psum_tensor("ps_kv", [128, 512], F32) as ps_kv:
                v_evac_last = 0
                t_kv = [0, 0]
                t_proj = [0, 0]
                v_acc = 0
                v_elu = [0] * NT
                dv = [None] * NT
                dv[0] = derive_pair_blocks(zpp[0][0], zpp[0][1], 0, v_z0, 0)

                def emit_kv(si):
                    nonlocal v_acc

                    def _(t, si=si, v_kn=v_elu[si], v_acc_=v_acc):
                        b.wait(t, "v", v_kn)
                        if si >= 1:
                            b.wait(t, "v", v_acc_)
                        kb, vb = kbp_[si % 2], vbp_[si % 2]
                        last = None
                        for h in range(NH):
                            lo = 64 * (h % 2)
                            last = t.matmul(
                                ps_kv[lo:lo + 64, (h // 2) * 64:(h // 2) * 64 + 64],
                                kb[:, h * 64:(h + 1) * 64],
                                vb[:, h * 64:(h + 1) * 64],
                                start=True, stop=True,
                                tile_position=(0, lo))
                        b.inc(last, "t", 1)
                    block.tensor(_)
                    t_kv[si % 2] = b.cnt["t"]

                    def _(v, si=si, tkv=t_kv[si % 2]):
                        b.wait(v, "t", tkv)
                        b.wait(v, "v")
                        if si == 0:
                            b.inc(v.tensor_copy(kv_s[:], ps_kv[:]), "v", 1)
                        else:
                            b.inc(v.tensor_add(kv_s[:], kv_s[:], ps_kv[:]), "v", 1)
                    block.vector(_)
                    v_acc = b.cnt["v"]

                for si in range(NT):
                    zh, zl = zpp[si % 2]

                    def _(t, si=si, v_evac_last=v_evac_last, spvp=dv[si]):
                        b.wait(t, "v", max(v_wk, spvp[1]))
                        if si >= 1:
                            b.wait(t, "v", v_evac_last)
                        tok_proj3(t, ps_v, wv_h, wv_l, zh, zl)
                        b.inc(tok_proj3(t, ps_k, wk_h, wk_l, zh, zl), "t", 1)
                    block.tensor(_)
                    t_proj[si % 2] = b.cnt["t"]
                    tp = b.cnt["t"]

                    if si >= 1:
                        emit_kv(si - 1)

                    def _(v, si=si, tp=tp, tkv=t_kv[si % 2]):
                        b.wait(v, "t", tp)
                        b.wait(v, "ld", ld_b)
                        b.wait(v, "t", tkv)
                        v.tensor_add(vbp_[si % 2], ps_v[:], bvb)
                        b.inc(v.tensor_add(kbp_[si % 2], ps_k[:], bkb), "v", 1)
                    block.vector(_)
                    v_evac_last = b.cnt["v"]
                    if si + 1 < NT:
                        dv[si + 1] = derive_pair_blocks(
                            zpp[(si + 1) % 2][0], zpp[(si + 1) % 2][1], si + 1,
                            v_z0, t_proj[(si + 1) % 2])
                    v_elu[si] = elu_norm(kbp_[si % 2])
                    for pj in range(4):
                        emit_prep_stripe(si * 4 + pj)
                emit_kv(NT - 1)
                g_wprep = b.cnt["gq"]

                def _(v):
                    b.wait(v, "v")
                    b.inc(v.tensor_scalar(kv_s[:], kv_s[:], 0.125, None, ALU.mult),
                          "v", 1)
                block.vector(_)
                def _(v):
                    b.wait(v, "v")
                    b.inc(v.tensor_copy(kvp_h, kv_s[:]), "v", 1)
                    b.wait(v, "v")
                    b.inc(v.tensor_sub(kvp_l, kv_s[:], kvp_h), "v", 1)
                block.vector(_)
            v_kv = b.cnt["v"]
            ph1s_cm.__exit__(None, None, None)
            wpb_cm.__exit__(None, None, None)
            wp_cm.__exit__(None, None, None)
            bkb_cm.__exit__(None, None, None)
            if phases < 2:
                if tap == "kv":
                    emit_tap(kv_s[:])
                return nc

            with nc.sbuf_tensor("ph1q", [128, 5376], F32) as PQ, \
                 nc.sbuf_tensor("ph1qw16", [128, 16384], BF16) as W16, \
                 nc.sbuf_tensor("ph1qp16", [128, 2048], BF16) as QP16:
                wq_h, wq_l = W16[:, 0:8192], W16[:, 8192:16384]
                qbp = [PQ[:, 0:1024], PQ[:, 4352:5376]]
                q1 = PQ[:, 1024:2048]
                q2 = PQ[:, 2048:3072]
                qrst = PQ[:, 3072:3200]
                bq_pq = PQ[:, 3328:4352]

                def elu_norm_q(blk):
                    s_entry = b.cnt["s"]

                    def _(v, s_entry=s_entry):
                        b.wait(v, "s", s_entry)
                        b.wait(v, "v")
                        b.inc(v.tensor_scalar(q1, blk, 0.0, None, ALU.min), "v", 1)
                    block.vector(_)
                    vv = b.cnt["v"]

                    def _(sc, vv=vv):
                        b.wait(sc, "v", vv)
                        b.inc(sc.activation(q2, q1, AF.Exp), "s", 1)
                    block.scalar(_)
                    sv = b.cnt["s"]

                    def _(v, sv=sv):
                        b.wait(v, "s", sv)
                        b.inc(v.tensor_scalar(q1, blk, 0.0, None, ALU.max), "v", 1)
                        b.wait(v, "v")
                        b.inc(v.scalar_tensor_tensor(blk, q2, -1.0, q1,
                                                     ALU.add, ALU.add), "v", 1)
                    block.vector(_)
                    vv = b.cnt["v"]

                    def _(sc, vv=vv):
                        b.wait(sc, "v", vv)
                        b.inc(sc.activation(q2, blk, AF.Square), "s", 1)
                    block.scalar(_)
                    sv = b.cnt["s"]

                    def _(v, sv=sv):
                        b.wait(v, "s", sv)
                        b.inc(v.reduce_sum(qrst[:, 0:16],
                                           q2.rearrange("p (h d) -> p h d", d=HD),
                                           axis=AX.X), "v", 1)
                    block.vector(_)
                    vv = b.cnt["v"]

                    def _(sc, vv=vv):
                        b.wait(sc, "v", vv)
                        b.inc(sc.activation(qrst[:, 16:32], qrst[:, 0:16], AF.Ln),
                              "s", 1)
                        b.wait(sc, "s")
                        b.inc(sc.activation(qrst[:, 16:32], qrst[:, 16:32], AF.Exp,
                                            scale=-0.5), "s", 1)
                    block.scalar(_)
                    sv = b.cnt["s"]

                    def _(v, sv=sv):
                        b.wait(v, "s", sv)
                        rb = qrst[:, 16:32].unsqueeze(2).broadcast_to([128, 16, HD])
                        b.inc(v.tensor_tensor(blk.rearrange("p (h d) -> p h d", d=HD),
                                              blk.rearrange("p (h d) -> p h d", d=HD),
                                              rb, ALU.mult), "v", 1)
                    block.vector(_)
                    return b.cnt["v"]

                bq_tile = bq_pq

                def _(sy, v_kv=v_kv):
                    b.wait(sy, "v", v_kv)
                    b.wait(sy, "gq", g_wprep)
                    b.dma(sy, "ld", out=bq_tile,
                          in_=bias_d["bq"].unsqueeze(0).partition_broadcast(128))
                block.sync(_)
                ld_bq = b.cnt["ld"]
                wtmp_q = PQ[:, 1024:3072]
                v_wq = load_resident_w_pairs(wq_h, wq_l, wq_d, wtmp_q, wait_v=v_kv)
                qp16 = QP16[:]

                with nc.psum_tensor("ps_q1", [128, 1024], F32) as ps_q, \
                     nc.psum_tensor("ps_trq16", [128, 2048], BF16) as ps_tq:
                    v_ev = 0
                    v_bias = 0
                    t_tr2 = [0, 0]
                    v_qn = [0] * NT

                    def emit_qtr(si):
                        nonlocal v_ev

                        def _(v, si=si, vq=v_qn[si]):
                            b.wait(v, "v", vq)
                            b.wait(v, "v")
                            qb = qbp[si % 2]
                            b.inc(v.tensor_copy(qp16[:, 0:1024], qb), "v", 1)
                            b.wait(v, "v")
                            b.inc(v.tensor_sub(qp16[:, 1024:2048], qb,
                                               qp16[:, 0:1024]), "v", 1)
                        block.vector(_)
                        vqp = b.cnt["v"]

                        def _(t, si=si, vqp=vqp, v_ev_=v_ev):
                            b.wait(t, "v", vqp)
                            if si > 0:
                                b.wait(t, "v", v_ev_)
                            last = None
                            for c in range(ND):
                                last = t.transpose(ps_tq[:, c * 128:(c + 1) * 128],
                                                   qp16[:, c * 128:(c + 1) * 128],
                                                   ident_b[:])
                            for c in range(ND):
                                last = t.transpose(
                                    ps_tq[:, 1024 + c * 128:1024 + (c + 1) * 128],
                                    qp16[:, 1024 + c * 128:1024 + (c + 1) * 128],
                                    ident_b[:])
                            b.inc(last, "t", 1)
                        block.tensor(_)
                        t_tr2[si % 2] = b.cnt["t"]
                        tv = b.cnt["t"]

                        def _(v, si=si, tv=tv):
                            b.wait(v, "t", tv)
                            dsth = B16[:].rearrange("p (c t) -> p c t", c=ND)[
                                :, :, si * 128:(si + 1) * 128]
                            b.inc(v.tensor_copy(
                                dsth, ps_tq[:, 0:1024].rearrange("p (c f) -> p c f", c=ND)),
                                "v", 1)
                            b.wait(v, "v")
                            dstl = B16[:].rearrange("p (c t) -> p c t", c=ND)[
                                :, :, S + si * 128:S + (si + 1) * 128]
                            b.inc(v.tensor_copy(
                                dstl, ps_tq[:, 1024:2048].rearrange("p (c f) -> p c f", c=ND)),
                                "v", 1)
                        block.vector(_)
                        v_ev = b.cnt["v"]

                    t_projq = [0, 0]
                    dvq = [None] * NT
                    dvq[0] = derive_pair_blocks(zpp[0][0], zpp[0][1], 0, v_kv, 0)
                    for si in range(NT):
                        zh, zl = zpp[si % 2]

                        def _(t, si=si, v_bias=v_bias, spvp=dvq[si]):
                            b.wait(t, "v", max(v_wq, spvp[1]))
                            b.wait(t, "ld", ld_bq)
                            if si >= 1:
                                b.wait(t, "v", v_bias)
                            b.inc(tok_proj3(t, ps_q, wq_h, wq_l, zh, zl), "t", 1)
                        block.tensor(_)
                        t_projq[si % 2] = b.cnt["t"]
                        tp = b.cnt["t"]

                        if si >= 1:
                            emit_qtr(si - 1)

                        def _(v, si=si, tp=tp, wtr=t_tr2[si % 2]):
                            b.wait(v, "t", tp)
                            b.wait(v, "t", wtr)
                            b.inc(v.tensor_add(qbp[si % 2], ps_q[:], bq_tile), "v", 1)
                        block.vector(_)
                        v_bias = b.cnt["v"]
                        if si + 1 < NT:
                            dvq[si + 1] = derive_pair_blocks(
                                zpp[(si + 1) % 2][0], zpp[(si + 1) % 2][1], si + 1,
                                v_kv, t_projq[(si + 1) % 2])
                        v_qn[si] = elu_norm_q(qbp[si % 2])
                    emit_qtr(NT - 1)
        v_qnT = b.cnt["v"]
        if phases < 3:
            if tap == "qnT":
                emit_tap(B16[:, 0:ND * S])
            return nc

        with nc.psum_tensor("ps_ctx", [128, S], F32) as ps_ctx:
            v_ev = 0
            for c in range(ND):
                def _(t, c=c, v_ev=v_ev):
                    b.wait(t, "v", v_qnT)
                    if c > 0:
                        b.wait(t, "v", v_ev)
                    last = None
                    for par in range(2):
                        h = 2 * c + par
                        lo = 64 * par
                        kc = slice((h // 2) * 64, (h // 2) * 64 + 64)
                        for j in range(4):
                            cols = slice(j * 512, (j + 1) * 512)
                            for pi, (kk, qq) in enumerate(
                                    ((kvp_h, Bp_hc(c)), (kvp_l, Bp_hc(c)),
                                     (kvp_h, Bp_lc(c)))):
                                last = t.matmul(
                                    ps_ctx[lo:lo + 64, cols],
                                    kk[lo:lo + 64, kc],
                                    qq[lo:lo + 64, cols],
                                    start=(pi == 0), stop=(pi == 2),
                                    tile_position=(lo, lo))
                    b.inc(last, "t", 1)
                block.tensor(_)
                tv = b.cnt["t"]

                def _(v, c=c, tv=tv):
                    b.wait(v, "t", tv)
                    b.wait(v, "v")
                    b.inc(v.tensor_copy(Bp_hc(c), ps_ctx[:]), "v", 1)
                    b.wait(v, "v")
                    b.inc(v.tensor_sub(Bp_lc(c), ps_ctx[:], Bp_hc(c)), "v", 1)
                block.vector(_)
                v_ev = b.cnt["v"]
        v_ctx = b.cnt["v"]
        if phases < 4:
            if tap == "ctxT":
                emit_tap(B16[:, 0:ND * S])
            return nc

        with nc.sbuf_tensor("ph3w16", [128, 16384], BF16) as WO16, \
             nc.sbuf_tensor("ph3wt", [128, 2048], F32) as WOT:
            wo_h, wo_l = WO16[:, 0:8192], WO16[:, 8192:16384]
            wtmp_o = WOT[:]
            v_wo = load_resident_w_pairs(wo_h, wo_l, wo_d, wtmp_o, wait_v=v_ctx)
            Bph = Bp_hc
            Bpl = Bp_lc

            with nc.psum_tensor("ps_wo", [128, S], F32) as ps_wo:
                v_ev = 0
                for n in range(ND):
                    def _(t, n=n, v_ev=v_ev):
                        b.wait(t, "v", max(v_wo, v_ctx))
                        if n > 0:
                            b.wait(t, "v", v_ev)
                        last = None
                        woh3 = wo_h.rearrange("p (k w) -> p k w", k=ND)
                        wol3 = wo_l.rearrange("p (k w) -> p k w", k=ND)
                        for j in range(4):
                            cols = slice(j * 512, (j + 1) * 512)
                            for pi, (wfn, bfn) in enumerate(
                                    ((woh3, Bph), (woh3, Bpl), (wol3, Bph))):
                                for e_ in range(ND):
                                    t.matmul(ps_wo[:, cols],
                                             wfn[:, e_, n * 128:(n + 1) * 128],
                                             bfn(e_)[:, cols],
                                             start=(pi == 0 and e_ == 0), stop=False)
                            last = t.matmul(ps_wo[:, cols], ident[:],
                                            A_c(n)[:, cols], start=False, stop=True)
                        b.inc(last, "t", 1)
                    block.tensor(_)
                    tv = b.cnt["t"]

                    def _(v, n=n, tv=tv):
                        b.wait(v, "t", tv)
                        b.inc(v.tensor_scalar(A_c(n), ps_wo[:], pc("bo", n), None,
                                              ALU.add), "v", 1)
                    block.vector(_)
                    v_ev = b.cnt["v"]
        v_pre1 = b.cnt["v"]
        if phases < 5:
            if tap == "pre1T":
                emit_tap(A_t[:])
            return nc

        def ln_stats_full(eps, v_ready, mu_row, rs_row, bc_mu, bc_rs, sq):
            with nc.psum_tensor(f"ps_s1_{b.cnt['t']}", [1, S], F32) as ps1, \
                 nc.psum_tensor(f"ps_s2_{b.cnt['t']}", [1, S], F32) as ps2:
                for c in range(ND):
                    tprev = b.cnt["t"]

                    def _(sc, c=c, tprev=tprev):
                        b.wait(sc, "v", v_ready)
                        if c >= 1:
                            b.wait(sc, "t", tprev)
                        b.inc(sc.activation(sq, A_c(c), AF.Square), "s", 1)
                    block.scalar(_)
                    sv = b.cnt["s"]

                    def _(t, c=c, sv=sv):
                        b.wait(t, "v", v_ready)
                        b.wait(t, "s", sv)
                        last = None
                        for j in range(4):
                            cols = slice(j * 512, (j + 1) * 512)
                            t.matmul(ps1[:, cols], ones_c[:], A_c(c)[:, cols],
                                     start=(c == 0), stop=(c == ND - 1))
                            last = t.matmul(ps2[:, cols], ones_c[:], sq[:, cols],
                                            start=(c == 0), stop=(c == ND - 1))
                        b.inc(last, "t", 1)
                    block.tensor(_)
                tst = b.cnt["t"]

                def _(v, tst=tst):
                    b.wait(v, "t", tst)
                    b.wait(v, "v")
                    b.inc(v.tensor_scalar(mu_row, ps1[:], INV_D, None, ALU.mult), "v", 1)
                    b.wait(v, "v")
                    b.inc(v.tensor_mul(rs_row, mu_row, mu_row), "v", 1)
                    b.wait(v, "v")
                    b.inc(v.scalar_tensor_tensor(rs_row, ps2[:], INV_D, rs_row,
                                                 ALU.mult, ALU.subtract), "v", 1)
                block.vector(_)
                vv = b.cnt["v"]

                def _(sc, vv=vv, eps=eps):
                    b.wait(sc, "v", vv)
                    bp = rs_row.base_partition()
                    b.wait(sc, "s")
                    b.inc(sc.activation(rs_row, rs_row, AF.Ln,
                                        bias=eps[bp:bp + 1, 0:1]), "s", 1)
                    b.wait(sc, "s")
                    b.inc(sc.activation(rs_row, rs_row, AF.Exp, scale=-0.5), "s", 1)
                block.scalar(_)
                v_rows = b.cnt["v"]

                def _(v):
                    b.wait(v, "s")
                    b.inc(v.engine_nop(), "v", 1)
                block.vector(_)
                v_rows = b.cnt["v"]

            if bc_mu is None:
                return v_rows
            with nc.psum_tensor(f"ps_bc_{b.cnt['t']}", [128, S], F32) as ps_bc:
                for src_row, dst_bc in [(mu_row, bc_mu), (rs_row, bc_rs)]:
                    vw = b.cnt["v"]
                    bp = src_row.base_partition()
                    one1 = ones_r[bp:bp + 1, :]

                    def _(t, src_row=src_row, vw=vw, v_rows=v_rows, one1=one1):
                        b.wait(t, "v", max(v_rows, vw))
                        last = None
                        for j in range(4):
                            cols = slice(j * 512, (j + 1) * 512)
                            last = t.matmul(ps_bc[:, cols], one1,
                                            src_row[:, cols], start=True, stop=True)
                        b.inc(last, "t", 1)
                    block.tensor(_)
                    tv = b.cnt["t"]

                    def _(v, dst_bc=dst_bc, tv=tv):
                        b.wait(v, "t", tv)
                        b.inc(v.tensor_copy(dst_bc, ps_bc[:]), "v", 1)
                    block.vector(_)
            return b.cnt["v"]

        with nc.sbuf_tensor("ph4", [128, 8192], F32) as P4:
            sq_s = P4[:, 0:2048]
            bc_mu = P4[:, 2048:4096]
            bc_rs = P4[:, 4096:6144]
            p1s = P4[:, 6144:8192]
            v_bc = ln_stats_full(epsc[:, 1:2], v_pre1, rows[0:1, :], rows[32:33, :],
                                 bc_mu, bc_rs, sq_s)
            for c in range(ND):
                def _(v, c=c, v_bc=v_bc):
                    b.wait(v, "v", v_bc)
                    b.wait(v, "v")
                    b.inc(v.tensor_sub(p1s, A_c(c), bc_mu), "v", 1)
                    b.wait(v, "v")
                    b.inc(v.scalar_tensor_tensor(p1s, p1s, pc("g1", c), bc_rs,
                                                 ALU.mult, ALU.mult), "v", 1)
                    b.wait(v, "v")
                    b.inc(v.tensor_scalar(A_c(c), p1s, pc("be1", c), None, ALU.add),
                          "v", 1)
                block.vector(_)
        v_attn = b.cnt["v"]


        if phases < 6:
            if tap == "attn_outT":
                emit_tap(A_t[:])
            return nc

        with nc.sbuf_tensor("ph5", [128, 9728], F32) as P5, \
             nc.sbuf_tensor("ph5p", [128, 12288], BF16) as P5P:
            wpair = [(P5P[:, 0:1024], P5P[:, 1024:2048]),
                     (P5P[:, 2048:3072], P5P[:, 3072:4096])]
            apg_h3 = P5P[:, 4096:8192].rearrange("p (c t) -> p c t", c=ND)
            apg_l3 = P5P[:, 8192:12288].rearrange("p (c t) -> p c t", c=ND)
            xTg = P5[:, 0:4096]
            pre2f = P5[:, 5632:9728]

            v_group_prev = v_attn

            for g in range(G):
                xr = P5[:, 5632:6656]
                with nc.psum_tensor(f"ps_trx_{g}", [128, 1024], F32) as ps_trx:
                    v_ev = 0
                    t_prev = 0
                    for tt in range(NTG):
                        def _(gp, g=g, tt=tt, t_prev=t_prev, v_group_prev=v_group_prev):
                            b.wait(gp, "t", t_prev)
                            b.wait(gp, "v", v_group_prev)
                            r0 = g * GS + tt * 128
                            b.dma(gp, "gq", out=xr, in_=x_d[r0:r0 + 128, :])
                        block.gpsimd(_)
                        gv = b.cnt["gq"]

                        def _(t, gv=gv, v_ev=v_ev):
                            b.wait(t, "gq", gv)
                            if v_ev:
                                b.wait(t, "v", v_ev)
                            last = None
                            for c in range(ND):
                                last = t.transpose(ps_trx[:, c * 128:(c + 1) * 128],
                                                   xr[:, c * 128:(c + 1) * 128],
                                                   ident[:])
                            b.inc(last, "t", 1)
                        block.tensor(_)
                        t_prev = b.cnt["t"]

                        def _(v, tt=tt, t_prev=t_prev):
                            b.wait(v, "t", t_prev)
                            dst = xTg.rearrange("p (c t) -> p c t", c=ND)[:, :, tt * 128:(tt + 1) * 128]
                            b.inc(v.tensor_copy(dst,
                                                ps_trx[:].rearrange("p (c f) -> p c f", c=ND)),
                                  "v", 1)
                        block.vector(_)
                        v_ev = b.cnt["v"]
                v_xT = b.cnt["v"]

                sA = b.cnt["s"]
                tA = b.cnt["t"]
                srcg = A_t[:].rearrange("p (c t) -> p c t", c=ND)[:, :, g * GS:(g + 1) * GS]

                def _(v, g=g, tA=tA, v_group_prev=v_group_prev):
                    b.wait(v, "v", v_attn)
                    b.wait(v, "v", v_group_prev)
                    b.wait(v, "v")
                    b.wait(v, "t", tA)
                    b.wait(v, "gq", g_wprep)
                    b.inc(v.tensor_copy(apg_h3, srcg), "v", 1)
                    b.wait(v, "v")
                    b.inc(v.tensor_sub(apg_l3, srcg, apg_h3), "v", 1)
                block.vector(_)
                v_apg = b.cnt["v"]
                s_apg = b.cnt["s"]

                with nc.psum_tensor(f"ps_h_{g}", [128, 1536], F32) as ps_h:
                    t_sweep0 = b.cnt["t"]
                    s_gelu = [0, 0]
                    v_gelu_last = 0
                    t_w1 = [t_sweep0, t_sweep0]
                    for i in range(NF):
                        def _(sy, i=i, ws_=t_w1[i % 2], wv_=(v_xT if i < 2 else 0),
                              vg_=(v_group_prev if i < 2 else 0)):
                            b.wait(sy, "t", ws_)
                            b.wait(sy, "v", max(wv_, vg_))
                            b.wait(sy, "gq", g_wprep)
                            wh, wl = wpair[i % 2]
                            for hl, dstv in ((0, wh), (1, wl)):
                                b.dma(sy, "ld", 
                                    out=dstv.rearrange("p (k c) -> p k c", k=ND),
                                    in_=w1p_d[hl, i].transpose([1, 0, 2]))
                        block.sync(_)
                        v_w1 = b.cnt["ld"]
                        sg = s_gelu[i % 2]

                        def _(t, g=g, i=i, v_w1=v_w1, sg=sg, v_apg=v_apg,
                              s_apg=s_apg, v_xT=v_xT):
                            b.wait(t, "ld", v_w1)
                            b.wait(t, "v", v_apg)
                            b.wait(t, "s", s_apg)
                            b.wait(t, "s", sg)
                            b.wait(t, "v", v_xT)
                            dst = ps_h[:, (i % 2) * 512:(i % 2) * 512 + 512]
                            wh, wl = wpair[i % 2]
                            last = None
                            for pi, (aa, ww) in enumerate(
                                    ((apg_h3, wh), (apg_l3, wh), (apg_h3, wl))):
                                for k in range(ND):
                                    last = t.matmul(dst, ww[:, k * 128:(k + 1) * 128],
                                                    aa[:, k, :],
                                                    start=(pi == 0 and k == 0),
                                                    stop=(pi == 2 and k == ND - 1))
                            b.inc(last, "t", 1)
                        block.tensor(_)
                        t_w1[i % 2] = b.cnt["t"]
                        tv = b.cnt["t"]

                        def _(sc, i=i, tv=tv, v_group_prev=v_group_prev,
                              vg=v_gelu_last):
                            b.wait(sc, "t", tv)
                            b.wait(sc, "v", v_group_prev)
                            b.wait(sc, "v", vg)
                            b.wait(sc, "s")
                            src = ps_h[:, (i % 2) * 512:(i % 2) * 512 + 512]
                            b.inc(sc.activation(ps_h[:, 1024:1536], src, AF.Gelu,
                                                bias=pc("b1", i)), "s", 1)
                        block.scalar(_)
                        sv2 = b.cnt["s"]

                        def _(v, i=i, sv2=sv2):
                            b.wait(v, "s", sv2)
                            b.wait(v, "v")
                            b.inc(v.tensor_copy(B16[:, i * 1024:i * 1024 + 512],
                                                ps_h[:, 1024:1536]), "v", 1)
                            b.wait(v, "v")
                            b.inc(v.tensor_sub(B16[:, i * 1024 + 512:i * 1024 + 1024],
                                               ps_h[:, 1024:1536],
                                               B16[:, i * 1024:i * 1024 + 512]),
                                  "v", 1)
                        block.vector(_)
                        v_gelu_last = b.cnt["v"]
                        s_gelu[i % 2] = b.cnt["s"]
                s_h = b.cnt["s"]
                v_h = b.cnt["v"]

                with nc.psum_tensor(f"ps_p2_{g}", [128, ND * 512], F32) as ps_p2:
                    t_sw2 = b.cnt["t"]
                    t_w2 = [t_sw2, t_sw2]
                    for i in range(NF):
                        def _(sy, i=i, ws_=t_w2[i % 2], wv_=(v_h if i < 2 else 0)):
                            b.wait(sy, "t", ws_)
                            b.wait(sy, "v", wv_)
                            b.wait(sy, "gq", g_wprep)
                            wh, wl = wpair[i % 2]
                            for hl, dstv in ((0, wh), (1, wl)):
                                b.dma(sy, "ld", out=dstv, in_=w2p_d[hl, i])
                        block.sync(_)
                        v_w2 = b.cnt["ld"]

                        def _(t, g=g, i=i, v_w2=v_w2, s_h=s_h, v_h=v_h):
                            b.wait(t, "ld", v_w2)
                            b.wait(t, "v", v_h)
                            b.wait(t, "s", s_h)
                            wh, wl = wpair[i % 2]
                            hh = B16[:, i * 1024:i * 1024 + 512]
                            hl = B16[:, i * 1024 + 512:i * 1024 + 1024]
                            last = None
                            for n in range(ND):
                                for pi, (mm, ww) in enumerate(
                                        ((hh, wh), (hl, wh), (hh, wl))):
                                    last = t.matmul(
                                        ps_p2[:, n * 512:(n + 1) * 512],
                                        ww[:, n * 128:(n + 1) * 128], mm,
                                        start=(i == 0 and pi == 0), stop=False)
                            if i == NF - 1:
                                for n in range(ND):
                                    last = t.matmul(ps_p2[:, n * 512:(n + 1) * 512],
                                                    ident[:],
                                                    A_c(n)[:, g * GS:(g + 1) * GS],
                                                    start=False, stop=True)
                            b.inc(last, "t", 1)
                        block.tensor(_)
                        t_w2[i % 2] = b.cnt["t"]
                    t_p2 = b.cnt["t"]

                    for n in range(ND):
                        def _(v, n=n, t_p2=t_p2):
                            b.wait(v, "t", t_p2)
                            b.inc(v.tensor_scalar(pre2f[:, n * 512:(n + 1) * 512],
                                                  ps_p2[:, n * 512:(n + 1) * 512],
                                                  pc("b2", n), None, ALU.add), "v", 1)
                        block.vector(_)
                    v_p2 = b.cnt["v"]

                sqg = P5[:, 4096:4608]
                mu2b = P5[:, 4608:5120]
                rs2b = P5[:, 5120:5632]
                r512a = rows[0:1, 0:512]
                r512b = rows[32:33, 0:512]
                with nc.psum_tensor(f"ps_sg1_{g}", [1, 512], F32) as psg1, \
                     nc.psum_tensor(f"ps_sg2_{g}", [1, 512], F32) as psg2:
                    for n in range(ND):
                        tprev = b.cnt["t"]

                        def _(sc, n=n, v_p2=v_p2, tprev=tprev):
                            b.wait(sc, "v", v_p2)
                            if n >= 1:
                                b.wait(sc, "t", tprev)
                            b.inc(sc.activation(sqg, pre2f[:, n * 512:(n + 1) * 512],
                                                AF.Square), "s", 1)
                        block.scalar(_)
                        sv = b.cnt["s"]

                        def _(t, n=n, sv=sv, v_p2=v_p2):
                            b.wait(t, "s", sv)
                            b.wait(t, "v", v_p2)
                            t.matmul(psg1[:], ones_c[:],
                                     pre2f[:, n * 512:(n + 1) * 512],
                                     start=(n == 0), stop=(n == ND - 1))
                            b.inc(t.matmul(psg2[:], ones_c[:], sqg,
                                           start=(n == 0), stop=(n == ND - 1)), "t", 1)
                        block.tensor(_)
                    tst = b.cnt["t"]

                    def _(v, tst=tst):
                        b.wait(v, "t", tst)
                        b.wait(v, "v")
                        b.inc(v.tensor_scalar(r512a, psg1[:], INV_D, None, ALU.mult), "v", 1)
                        b.wait(v, "v")
                        b.inc(v.tensor_mul(r512b, r512a, r512a), "v", 1)
                        b.wait(v, "v")
                        b.inc(v.scalar_tensor_tensor(r512b, psg2[:], INV_D, r512b,
                                                     ALU.mult, ALU.subtract), "v", 1)
                    block.vector(_)
                    vv = b.cnt["v"]

                    def _(sc, vv=vv):
                        b.wait(sc, "v", vv)
                        b.wait(sc, "s")
                        b.inc(sc.activation(r512b, r512b, AF.Ln,
                                            bias=epsc[32:33, 1:2]), "s", 1)
                        b.wait(sc, "s")
                        b.inc(sc.activation(r512b, r512b, AF.Exp, scale=-0.5), "s", 1)
                    block.scalar(_)
                    sv = b.cnt["s"]

                    def _(v, sv=sv):
                        b.wait(v, "s", sv)
                        b.inc(v.engine_nop(), "v", 1)
                    block.vector(_)
                    v_rows2 = b.cnt["v"]

                with nc.psum_tensor(f"ps_bcg_{g}", [128, 1024], F32) as ps_bcg:
                    def _(t, v_rows2=v_rows2):
                        b.wait(t, "v", v_rows2)
                        t.matmul(ps_bcg[:, 0:512], ones_r[0:1, :], r512a,
                                 start=True, stop=True)
                        b.inc(t.matmul(ps_bcg[:, 512:1024], ones_r[32:33, :], r512b,
                                       start=True, stop=True), "t", 1)
                    block.tensor(_)
                    tv = b.cnt["t"]

                    def _(v, tv=tv):
                        b.wait(v, "t", tv)
                        v.tensor_copy(mu2b, ps_bcg[:, 0:512])
                        b.inc(v.tensor_copy(rs2b, ps_bcg[:, 512:1024]), "v", 1)
                    block.vector(_)
                    v_bc2 = b.cnt["v"]

                for n in range(ND):
                    def _(v, n=n, g=g, v_bc2=v_bc2, v_xT=v_xT):
                        b.wait(v, "v", v_bc2)
                        b.wait(v, "v", v_xT)
                        b.wait(v, "v")
                        b.inc(v.tensor_sub(sqg, pre2f[:, n * 512:(n + 1) * 512],
                                           mu2b), "v", 1)
                        b.wait(v, "v")
                        b.inc(v.scalar_tensor_tensor(sqg, sqg, pc("g2", n), rs2b,
                                                     ALU.mult, ALU.mult), "v", 1)
                        b.wait(v, "v")
                        xc = xTg.rearrange("p (c t) -> p c t", c=ND)[:, n, :]
                        b.inc(v.scalar_tensor_tensor(
                            A_c(n)[:, g * GS:(g + 1) * GS], sqg,
                            pc("be2", n), xc, ALU.add, ALU.add), "v", 1)
                    block.vector(_)
                v_group_prev = b.cnt["v"]
        v_x1 = b.cnt["v"]
        if phases < 7:
            if tap == "x1T":
                emit_tap(A_t[:])
            return nc

        with nc.sbuf_tensor("ph6", [128, 2048], F32) as P6:
            ln_stats_full(epsc[:, 0:1], v_x1, rows[0:1, :], rows[32:33, :],
                          None, None, P6[:, 0:2048])
        v_ln0b = b.cnt["v"]

        ug_all = en(nc.sbuf_tensor("ug_all", [128, 512], F32))
        ph7_cm = nc.sbuf_tensor("ph7", [128, 11266], F32)
        P7 = ph7_cm.__enter__()
        ph7b_cm = nc.sbuf_tensor("ph7b", [128, 8192], BF16)
        P7B = ph7b_cm.__enter__()
        ttf4 = P7[:, 0:4096]
        abs4 = P7[:, 4096:8192]
        ab_s = P7[:, 8192:9216]
        wfbuf = [P7[:, 9216:10240], P7[:, 10240:11264]]
        thr_s = P7[:, 11264:11266]
        tern_b = P7B[:, 0:4096]
        evac_fc = [P7B[:, 4096:5120], P7B[:, 5120:6144]]
        evac_pj = [P7B[:, 6144:7168], P7B[:, 7168:8192]]

        with nc.psum_tensor("ps_trf", [128, 1024], BF16) as ps_trf:
            g_fc = [0, 0]
            v_fcg = [0, 0]
            s_fcg = [0, 0]
            t_trf = 0
            for si in range(NF):
                wb_ = wfbuf[si % 2]

                def _(sy, si=si, wb_=wb_, wv_=v_fcg[si % 2], ws_=s_fcg[si % 2]):
                    b.wait(sy, "v", max(wv_, v_ln0b))
                    b.wait(sy, "s", ws_)
                    for hh in range(2):
                        b.dma(sy, "ld", out=wb_[:, hh * 512:(hh + 1) * 512],
                              in_=wfc_d[si * 128:(si + 1) * 128,
                                        hh * 512:(hh + 1) * 512])
                block.sync(_)
                ldv = b.cnt["ld"]
                vprev_abs = b.cnt["v"]

                def _(sc, si=si, wb_=wb_, ldv=ldv, vprev_abs=vprev_abs):
                    b.wait(sc, "ld", ldv)
                    b.wait(sc, "v", vprev_abs)
                    b.inc(sc.activation(ab_s, wb_, AF.Abs,
                                        accum_out=sc_fc[:, si:si + 1]), "s", 1)
                block.scalar(_)
                sv = b.cnt["s"]
                s_fcg[si % 2] = sv

                def _(v, si=si, wb_=wb_, sv=sv, t_trf=t_trf):
                    b.wait(v, "s", sv)
                    b.wait(v, "t", t_trf)
                    b.wait(v, "v")
                    b.inc(v.tensor_scalar(sc_fc[:, si:si + 1], sc_fc[:, si:si + 1],
                                          INV_D, None, ALU.mult), "v", 1)
                    b.wait(v, "v")
                    b.inc(v.tensor_scalar(sc_fc[:, si:si + 1], sc_fc[:, si:si + 1],
                                          1e-5, None, ALU.max), "v", 1)
                    b.wait(v, "v")
                    v.tensor_scalar(thr_s[:, 0:1], sc_fc[:, si:si + 1], 0.5,
                                    None, ALU.mult)
                    b.inc(v.tensor_scalar(thr_s[:, 1:2], sc_fc[:, si:si + 1], -0.5,
                                          None, ALU.mult), "v", 1)
                    b.wait(v, "v")
                    b.inc(v.tensor_scalar(ab_s, wb_, thr_s[:, 0:1], None, ALU.is_gt),
                          "v", 1)
                    b.wait(v, "v")
                    b.inc(v.scalar_tensor_tensor(
                        tern_b[:, 0:1024],
                        wb_, thr_s[:, 1:2], ab_s, ALU.is_lt, ALU.subtract), "v", 1)
                block.vector(_)
                vv = b.cnt["v"]
                v_fcg[si % 2] = vv

                def _(t, vv=vv):
                    b.wait(t, "v", vv)
                    last = None
                    for k in range(ND):
                        last = t.transpose(ps_trf[:, k * 128:(k + 1) * 128],
                                           tern_b[:, k * 128:(k + 1) * 128],
                                           ident_b[:])
                    b.inc(last, "t", 1)
                block.tensor(_)
                t_trf = b.cnt["t"]

                ev_ = evac_fc[si % 2]

                def _(v, t_trf=t_trf, gw=g_fc[si % 2], ev_=ev_):
                    b.wait(v, "t", t_trf)
                    b.wait(v, "gq", gw)
                    b.inc(v.tensor_copy(ev_, ps_trf[:]), "v", 1)
                block.vector(_)
                vv = b.cnt["v"]

                def _(gp, si=si, vv=vv, ev_=ev_):
                    b.wait(gp, "v", vv)
                    b.dma(gp, "gq", out=sfc_d[si].transpose([1, 0, 2]),
                                       in_=ev_
                                       .rearrange("p (k c) -> p k c", k=ND))
                block.gpsimd(_)
                g_fc[si % 2] = b.cnt["gq"]

            g_pj = [0, 0]
            v_pjt = b.cnt["v"]
            for j in range(ND):

                def _(sy, j=j, v_pjt=v_pjt):
                    b.wait(sy, "v", v_pjt)
                    for qq in range(4):
                        b.dma(sy, "ld",
                              out=ttf4[:, qq * 1024:(qq + 1) * 1024],
                              in_=wpj_d[j * 128:(j + 1) * 128,
                                        qq * 1024:(qq + 1) * 1024])
                block.sync(_)
                ldv = b.cnt["ld"]

                vprev_abs = b.cnt["v"]

                def _(sc, j=j, ldv=ldv, vprev_abs=vprev_abs):
                    b.wait(sc, "ld", ldv)
                    b.wait(sc, "v", vprev_abs)
                    b.inc(sc.activation(abs4, ttf4, AF.Abs,
                                        accum_out=sc_pj[:, j:j + 1]), "s", 1)
                block.scalar(_)
                sv = b.cnt["s"]
                tprev = b.cnt["t"]

                def _(v, j=j, sv=sv, tprev=tprev):
                    b.wait(v, "s", sv)
                    b.wait(v, "t", tprev)
                    b.wait(v, "v")
                    b.inc(v.tensor_scalar(sc_pj[:, j:j + 1], sc_pj[:, j:j + 1],
                                          INV_FF, None, ALU.mult), "v", 1)
                    b.wait(v, "v")
                    b.inc(v.tensor_scalar(sc_pj[:, j:j + 1], sc_pj[:, j:j + 1],
                                          1e-5, None, ALU.max), "v", 1)
                    b.wait(v, "v")
                    v.tensor_scalar(thr_s[:, 0:1], sc_pj[:, j:j + 1], 0.5, None, ALU.mult)
                    b.inc(v.tensor_scalar(thr_s[:, 1:2], sc_pj[:, j:j + 1], -0.5,
                                          None, ALU.mult), "v", 1)
                    for q in range(4):
                        cols = slice(q * 1024, (q + 1) * 1024)
                        b.wait(v, "v")
                        b.inc(v.tensor_scalar(ab_s, ttf4[:, cols], thr_s[:, 0:1],
                                              None, ALU.is_gt), "v", 1)
                        b.wait(v, "v")
                        b.inc(v.scalar_tensor_tensor(
                            tern_b[:, q * 1024:(q + 1) * 1024],
                            ttf4[:, cols], thr_s[:, 1:2], ab_s,
                            ALU.is_lt, ALU.subtract), "v", 1)
                block.vector(_)
                vv = b.cnt["v"]
                v_pjt = vv

                for q in range(4):
                    def _(t, q=q, vv=vv):
                        b.wait(t, "v", b.cnt["v"])
                        last = None
                        for kk in range(ND):
                            i_blk = q * ND + kk
                            last = t.transpose(ps_trf[:, kk * 128:(kk + 1) * 128],
                                               tern_b[:, i_blk * 128:(i_blk + 1) * 128],
                                               ident_b[:])
                        b.inc(last, "t", 1)
                    block.tensor(_)
                    tv = b.cnt["t"]
                    ev_ = evac_pj[q % 2]

                    def _(v, tv=tv, gw=g_pj[q % 2], ev_=ev_):
                        b.wait(v, "t", tv)
                        b.wait(v, "gq", gw)
                        b.inc(v.tensor_copy(ev_, ps_trf[:]), "v", 1)
                    block.vector(_)
                    vv2 = b.cnt["v"]

                    def _(gp, j=j, q=q, vv2=vv2, ev_=ev_):
                        b.wait(gp, "v", vv2)
                        dst = spj_d[j // 4, q * ND:(q + 1) * ND, j % 4]
                        b.dma(gp, "gq", out=dst.transpose([1, 0, 2]),
                                           in_=ev_
                                           .rearrange("p (i c) -> p i c", i=ND))
                    block.gpsimd(_)
                    g_pj[q % 2] = b.cnt["gq"]

            gprev = b.cnt["gq"]

            def _(sy, gprev=gprev):
                b.wait(sy, "gq", gprev)
                b.dma(sy, "ld", out=ug_all[:].rearrange("p (s r) -> p s r", r=16),
                                   in_=u_d.rearrange("(s p) r -> s p r", p=128).transpose([1, 0, 2]))
            block.sync(_)
            ld_u = b.cnt["ld"]

            def _(v, ld_u=ld_u):
                b.wait(v, "ld", ld_u)
                b.wait(v, "v")
                b.inc(v.reciprocal(gs_fc[:], sc_fc[:]), "v", 1)
                b.wait(v, "v")
                b.inc(v.tensor_scalar(gs_fc[:], gs_fc[:], gate_b[:, 0:1], None,
                                      ALU.mult), "v", 1)
                b.wait(v, "v")
                b.inc(v.tensor_scalar(gs_fc[:], gs_fc[:], -1.0, None, ALU.mult),
                      "v", 1)
                b.wait(v, "v")
                v.tensor_scalar(scn_fc[:], sc_fc[:], -1.0, None, ALU.mult)
                b.inc(v.tensor_scalar(scn_pj[:], sc_pj[:], -1.0, None, ALU.mult),
                      "v", 1)
            block.vector(_)
        v_prep = b.cnt["v"]
        ph7b_cm.__exit__(None, None, None)
        ph7_cm.__exit__(None, None, None)
        if phases < 8:
            if tap == "prep":
                emit_tap(sc_fc[:])
            return nc

        with nc.sbuf_tensor("ph8", [128, 6672], F32) as P8, \
             nc.sbuf_tensor("ph8z", [128, 8192], BF16) as P8Z, \
             nc.sbuf_tensor("ph8u", [16, FF], F32) as uTg, \
             nc.sbuf_tensor("ph8w", [128, 2048], BF16) as WF, \
             nc.sbuf_tensor("ph8vp", [128, 256], BF16) as VP:
            yTg = P8[:, 0:4096]
            t1T = P8[:, 4096:4608]
            t2T = P8[:, 4608:5120]
            bcz = P8[:, 5120:6144]
            ytm = P8[:, 5120:6144]
            z1tmp = P8[:, 6144:6656]
            usm = P8[:, 6656:6672]
            fcb = [WF[:, 0:1024], WF[:, 1024:2048]]
            pjb2 = [WF[:, 0:1024], WF[:, 1024:2048]]
            vp_h, vp_l = VP[:, 0:128], VP[:, 128:256]

            def z1h(k):
                return P8Z[:, k * 1024:k * 1024 + 512]

            def z1l(k):
                return P8Z[:, k * 1024 + 512:(k + 1) * 1024]

            sE, vE = b.cnt["s"], b.cnt["v"]

            def _(v):
                b.wait(v, "v", v_prep)
                b.wait(v, "v")
                b.inc(v.tensor_copy(vp_h, vT_s[:]), "v", 1)
                b.wait(v, "v")
                b.inc(v.tensor_sub(vp_l, vT_s[:], vp_h), "v", 1)
            block.vector(_)
            v_vp = b.cnt["v"]

            with nc.psum_tensor("ps_u", [128, 128], F32) as ps_u:
                v_prev = v_prep
                for st in range(NF):
                    def _(v, st=st, v_prev=v_prev):
                        b.wait(v, "v", v_prev)
                        b.inc(v.tensor_scalar(usm,
                                              ug_all[:, st * 16:(st + 1) * 16],
                                              gs_fc[:, st:st + 1], None, ALU.mult),
                              "v", 1)
                    block.vector(_)
                    vv = b.cnt["v"]

                    def _(t, st=st, vv=vv):
                        b.wait(t, "v", vv)
                        b.inc(t.transpose(ps_u[0:16, 0:128], usm, ident[:]),
                              "t", 1)
                    block.tensor(_)
                    tv = b.cnt["t"]

                    def _(v, st=st, tv=tv):
                        b.wait(v, "t", tv)
                        b.inc(v.tensor_copy(uTg[:, st * 128:(st + 1) * 128],
                                            ps_u[0:16, 0:128]), "v", 1)
                    block.vector(_)
                    v_prev = b.cnt["v"]
            v_u = b.cnt["v"]

            v_group_prev = v_u
            g_out = 0
            for g in range(G):
                with nc.psum_tensor(f"ps_bz_{g}", [128, 1024], F32) as ps_bz:
                    def _(t, g=g, v_group_prev=v_group_prev):
                        b.wait(t, "v", v_group_prev)
                        cols = slice(g * GS, (g + 1) * GS)
                        t.matmul(ps_bz[:, 0:512], ones_r[0:1, :], rows[0:1, cols],
                                 start=True, stop=True)
                        b.inc(t.matmul(ps_bz[:, 512:1024], ones_r[32:33, :],
                                       rows[32:33, cols],
                                       start=True, stop=True), "t", 1)
                    block.tensor(_)
                    tv = b.cnt["t"]
                    gw = g_out

                    def _(v, tv=tv, gw=gw):
                        b.wait(v, "t", tv)
                        b.wait(v, "gq", gw)
                        b.inc(v.tensor_copy(bcz, ps_bz[:]), "v", 1)
                    block.vector(_)
                    v_bcz = b.cnt["v"]

                for k in range(ND):
                    t_zf = b.cnt["t"]

                    def _(v, k=k, g=g, v_bcz=v_bcz, t_zf=t_zf):
                        b.wait(v, "v", v_bcz)
                        b.wait(v, "v")
                        if k == 0:
                            b.wait(v, "t", t_zf)
                        b.inc(v.tensor_sub(z1tmp, A_c(k)[:, g * GS:(g + 1) * GS],
                                           bcz[:, 0:512]), "v", 1)
                        b.wait(v, "v")
                        b.inc(v.tensor_mul(z1tmp, z1tmp, bcz[:, 512:1024]), "v", 1)
                    block.vector(_)
                    vv = b.cnt["v"]

                    def _(v, k=k, vv=vv):
                        b.wait(v, "v", vv)
                        b.inc(v.tensor_copy(z1h(k), z1tmp), "v", 1)
                        b.wait(v, "v")
                        b.inc(v.tensor_sub(z1l(k), z1tmp, z1h(k)), "v", 1)
                    block.vector(_)
                v_z1 = b.cnt["v"]

                with nc.psum_tensor(f"ps_lr_{g}", [16, 512], F32) as ps_lr:
                    def _(t, v_z1=v_z1):
                        b.wait(t, "v", max(v_z1, v_vp))
                        last = None
                        for pi, (vv_, zz) in enumerate(
                                ((vp_h, z1h), (vp_h, z1l), (vp_l, z1h))):
                            for k in range(ND):
                                last = t.matmul(ps_lr[:],
                                                vv_[:, k * 16:(k + 1) * 16],
                                                zz(k),
                                                start=(pi == 0 and k == 0),
                                                stop=(pi == 2 and k == ND - 1))
                        b.inc(last, "t", 1)
                    block.tensor(_)
                    tv = b.cnt["t"]

                    def _(v, tv=tv):
                        b.wait(v, "t", tv)
                        b.inc(v.tensor_copy(t1T[0:16, :], ps_lr[:]), "v", 1)
                    block.vector(_)
                    vv = b.cnt["v"]

                    def _(t, vv=vv):
                        b.wait(t, "v", vv)
                        b.inc(t.matmul(ps_lr[:], fbT[:], t1T[0:16, :],
                                       start=True, stop=True), "t", 1)
                    block.tensor(_)
                    tv = b.cnt["t"]

                    def _(v, tv=tv):
                        b.wait(v, "t", tv)
                        b.inc(v.tensor_copy(t2T[0:16, :], ps_lr[:]), "v", 1)
                    block.vector(_)
                    v_t2 = b.cnt["v"]

                with nc.psum_tensor(f"ps_h2_{g}", [128, 1536], F32) as ps_h2:
                    t_ent = b.cnt["t"]
                    s_rr = [0, 0]
                    v_rr_last = 0
                    t_fc = [t_ent, t_ent]
                    for i in range(NF):
                        ws_ = t_fc[i % 2]

                        def _(sy, i=i, ws_=ws_):
                            b.wait(sy, "t", ws_)
                            b.dma(sy, "ld", 
                                out=fcb[i % 2].rearrange("p (k c) -> p k c", k=ND),
                                in_=sfc_d[i].transpose([1, 0, 2]))
                        block.sync(_)
                        ldv = b.cnt["ld"]
                        sg = s_rr[i % 2]

                        def _(t, i=i, ldv=ldv, sg=sg, v_t2=v_t2, v_z1=v_z1):
                            b.wait(t, "ld", ldv)
                            b.wait(t, "v", max(v_t2, v_z1))
                            b.wait(t, "s", sg)
                            dst = ps_h2[:, (i % 2) * 512:(i % 2) * 512 + 512]
                            for pi, zz in enumerate((z1h, z1l)):
                                for k in range(ND):
                                    t.matmul(dst, fcb[i % 2][:, k * 128:(k + 1) * 128],
                                             zz(k), start=(pi == 0 and k == 0),
                                             stop=False)
                            b.inc(t.matmul(dst, uTg[:, i * 128:(i + 1) * 128],
                                           t2T[0:16, :], start=False, stop=True),
                                  "t", 1)
                        block.tensor(_)
                        t_fc[i % 2] = b.cnt["t"]
                        tv = b.cnt["t"]

                        def _(sc, i=i, tv=tv, v_group_prev=v_group_prev,
                              vr=v_rr_last):
                            b.wait(sc, "t", tv)
                            b.wait(sc, "v", v_group_prev)
                            b.wait(sc, "v", vr)
                            b.wait(sc, "s")
                            src = ps_h2[:, (i % 2) * 512:(i % 2) * 512 + 512]
                            b.inc(sc.activation(bcz[:, 0:512], src, AF.Relu,
                                                scale=scn_fc[:, i:i + 1]), "s", 1)
                            b.wait(sc, "s")
                            b.inc(sc.activation(ps_h2[:, 1024:1536], bcz[:, 0:512],
                                                AF.Square), "s", 1)
                        block.scalar(_)
                        sv2 = b.cnt["s"]

                        def _(v, i=i, sv2=sv2):
                            b.wait(v, "s", sv2)
                            b.wait(v, "v")
                            b.inc(v.tensor_copy(B16[:, i * 1024:i * 1024 + 512],
                                                ps_h2[:, 1024:1536]), "v", 1)
                            b.wait(v, "v")
                            b.inc(v.tensor_sub(B16[:, i * 1024 + 512:i * 1024 + 1024],
                                               ps_h2[:, 1024:1536],
                                               B16[:, i * 1024:i * 1024 + 512]),
                                  "v", 1)
                        block.vector(_)
                        v_rr_last = b.cnt["v"]
                        s_rr[i % 2] = b.cnt["s"]
                s_rrd = b.cnt["s"]
                v_rrd = b.cnt["v"]

                t_y_half = []
                for half in range(2):
                    with nc.psum_tensor(f"ps_y_{g}_{half}", [128, 2048], F32) as ps_y:
                        t_half0 = b.cnt["t"]
                        v_half0 = b.cnt["v"]
                        t_pj = [t_half0, t_half0]
                        ldv = 0
                        for i in range(NF):
                            if i % 2 == 0:
                                par = (i // 2) % 2

                                def _(sy, half=half, i=i, par=par, ws_=t_pj[par],
                                      s_rrd=s_rrd):
                                    b.wait(sy, "t", ws_)
                                    if half == 0 and i < 4:
                                        b.wait(sy, "s", s_rrd)
                                    src2 = spj_d[half, i:i + 2] \
                                        .rearrange("i2 jj p c -> (i2 jj) p c") \
                                        .transpose([1, 0, 2])
                                    b.dma(sy, "ld",
                                          out=pjb2[par].rearrange("p (m c) -> p m c", m=8),
                                          in_=src2)
                                block.sync(_)
                                ldv = b.cnt["ld"]

                            def _(t, i=i, ldv=ldv, s_rrd=s_rrd, v_half0=v_half0,
                                  v_rrd=v_rrd):
                                b.wait(t, "ld", ldv)
                                b.wait(t, "s", s_rrd)
                                b.wait(t, "v", v_rrd)
                                b.wait(t, "v", v_half0)
                                rrh = B16[:, i * 1024:i * 1024 + 512]
                                rrl = B16[:, i * 1024 + 512:i * 1024 + 1024]
                                buf = pjb2[(i // 2) % 2]
                                off = (i % 2) * 512
                                last = None
                                for jj in range(4):
                                    for pi, mm in enumerate((rrh, rrl)):
                                        last = t.matmul(
                                            ps_y[:, jj * 512:(jj + 1) * 512],
                                            buf[:, off + jj * 128:off + (jj + 1) * 128],
                                            mm, start=(i == 0 and pi == 0),
                                            stop=(i == NF - 1 and pi == 1))
                                b.inc(last, "t", 1)
                            block.tensor(_)
                            t_pj[(i // 2) % 2] = b.cnt["t"]
                        t_y = b.cnt["t"]

                        for jj in range(4):
                            j = half * 4 + jj

                            def _(v, g=g, j=j, jj=jj, t_y=t_y, v_z1=v_z1):
                                b.wait(v, "t", t_y)
                                b.wait(v, "v", v_z1)
                                b.inc(v.scalar_tensor_tensor(
                                    yTg[:, j * 512:(j + 1) * 512],
                                    ps_y[:, jj * 512:(jj + 1) * 512],
                                    scn_pj[:, j:j + 1],
                                    A_c(j)[:, g * GS:(g + 1) * GS],
                                    ALU.mult, ALU.add), "v", 1)
                            block.vector(_)
                v_y = b.cnt["v"]

                with nc.psum_tensor(f"ps_ty_{g}", [128, 1024], F32) as ps_ty:
                    v_prev_ty = 0
                    for tt in range(NTG):
                        def _(t, tt=tt, v_y=v_y, v_prev_ty=v_prev_ty):
                            b.wait(t, "v", v_y)
                            if v_prev_ty:
                                b.wait(t, "v", v_prev_ty)
                            last = None
                            for c in range(ND):
                                last = t.transpose(
                                    ps_ty[:, c * 128:(c + 1) * 128],
                                    yTg[:, c * 512 + tt * 128: c * 512 + (tt + 1) * 128],
                                    ident[:])
                            b.inc(last, "t", 1)
                        block.tensor(_)
                        tv = b.cnt["t"]
                        gw = g_out

                        def _(v, tv=tv, gw=gw):
                            b.wait(v, "t", tv)
                            b.wait(v, "gq", gw)
                            b.inc(v.tensor_copy(ytm, ps_ty[:]), "v", 1)
                        block.vector(_)
                        v_prev_ty = b.cnt["v"]

                        def _(gp, g=g, tt=tt, v_prev_ty=v_prev_ty):
                            b.wait(gp, "v", v_prev_ty)
                            r0 = g * GS + tt * 128
                            b.dma(gp, "gq", out=y_d[r0:r0 + 128, :], in_=ytm)
                        block.gpsimd(_)
                        g_out = b.cnt["gq"]
                v_group_prev = b.cnt["v"]

            def _(gp):
                b.wait(gp, "gq")
            block.gpsimd(_)
        return nc


_W_NAMES = [("wq", "L0_attn_Wq"), ("wk", "L0_attn_Wk"), ("wv", "L0_attn_Wv"),
            ("wo", "L0_attn_Wo"), ("w2", "L0_attn_W2"), ("wfc", "L0_ffn_w_fc"),
            ("wpj", "L0_ffn_w_proj"), ("u", "L0_ffn_u"), ("vw", "L0_ffn_v"),
            ("fb", "L0_ffn_fast_b"),
            ("bq", "L0_attn_bq"), ("bk", "L0_attn_bk"), ("bv", "L0_attn_bv"),
            ("bo", "L0_attn_bo"), ("b1", "L0_attn_b1"), ("b2", "L0_attn_b2"),
            ("g1", "L0_attn_g1"), ("be1", "L0_attn_be1"), ("g2", "L0_attn_g2"),
            ("be2", "L0_attn_be2")]


def make_inputs(kw):
    common = {}
    for short, full in _W_NAMES:
        common[short] = np.ascontiguousarray(np.asarray(kw[full], np.float32))
    w1 = np.asarray(kw["L0_attn_W1"], np.float32)
    common["w1t"] = np.ascontiguousarray(
        w1.reshape(ND, 128, NF, 128).transpose(2, 0, 1, 3))
    common["gate"] = np.ascontiguousarray(
        np.asarray(kw["L0_ffn_gate"], np.float32).reshape(1, 1))
    common["ident_in"] = np.eye(128, dtype=np.float32)
    x = np.asarray(kw["L0_attn_x"], np.float32)
    return common, x


def kernel(**kw):
    common, x = make_inputs(kw)
    nc = build_program(tap=DEBUG_TAP, phases=BUILD_PHASES)
    in_maps = [dict(common, x=np.ascontiguousarray(x[bb])) for bb in range(8)]
    res = run_bass_kernel_spmd(nc, in_maps, core_ids=list(range(8)))
    return np.stack([r["y"] for r in res.results], axis=0)
